# version 02
# baseline (speedup 1.0000x reference)
import numpy as np
import ml_dtypes
import concourse.bass as bass
import concourse.mybir as mybir
from concourse.bass_utils import run_bass_kernel_spmd
from contextlib import ExitStack

F32 = mybir.dt.float32
BF16 = mybir.dt.bfloat16
I32 = mybir.dt.int32
AF = mybir.ActivationFunctionType
ALU = mybir.AluOpType

NCORE = 8
S = 4096
D = 1024
INNER = 2048
NH = 4
DH = 512
HID = 3072
R = 1024
GH = 16384
TILE = 512
NT = S // TILE
LAM = 0.5 * float(np.log(DH))

FUSED = True
DEBUG_B = False
SERIAL_PHASES = set()
B_SER = ""
DEBUG_B_TILE = (3, 7)
PHASES = None
CORES = None

ENGINES = ("tensor", "vector", "scalar", "gpsimd", "sync")
DQUEUE = {"dsync": "sync", "dscalar": "scalar", "dgp": "gpsimd"}


class Tok:
    __slots__ = ("sem", "val")
    def __init__(self, sem, val):
        self.sem = sem
        self.val = val


class Emit:
    def __init__(self, nc, es):
        self.nc = nc
        self.ops = {e: [] for e in ENGINES}
        self.sem = {}
        self.cnt = {}
        for e in ENGINES:
            self.sem[e] = es.enter_context(nc.semaphore(f"sp_{e}"))
            self.cnt[e] = 0
        for q in DQUEUE:
            self.sem[q] = es.enter_context(nc.semaphore(f"sq_{q}"))
            self.cnt[q] = 0
        self.bar = es.enter_context(nc.semaphore("sbar"))
        self.bar_cnt = 0
        self.fences = {e: [] for e in ("vector", "scalar", "gpsimd")}
        self.fencebuf = es.enter_context(nc.sbuf_tensor("fencebuf", [1, 8], F32))
        self.ops["vector"].append(("o", lambda e: e.memset(self.fencebuf[:, :], 0.0), None))
        self.vc = {e: {s: 0 for s in self.sem} for e in ENGINES}

    FENCED = ("vector", "scalar", "gpsimd")

    def _fence_target(self, peng, val):
        for cov, fval in self.fences[peng]:
            if cov >= val:
                return fval
        cov = self.cnt[peng]
        self.cnt[peng] += 1
        fval = self.cnt[peng]
        fb = self.fencebuf
        if peng == "scalar":
            fn = lambda e: e.mul(fb[0:1, 4:8], fb[0:1, 0:4], 1.0)
        else:
            fn = lambda e: e.memset(fb[0:1, 0:4], 0.0)
        self.ops[peng].append(("o", fn, peng))
        self.fences[peng].append((cov, fval))
        return fval

    def _waits(self, eng, deps, dma=False):
        for t in deps:
            if t is None:
                continue
            if t.sem == eng and not dma:
                continue
            v = t.val
            s = t.sem
            if s == "dgp":
                v = self.cnt["dgp"]
            elif s in Emit.FENCED and (s != eng or dma):
                v = self._fence_target(s, t.val)
            if self.vc[eng][s] < v:
                self.vc[eng][s] = v
                self.ops[eng].append(("w", s, v))

    SERIAL = False

    def on(self, eng, fn, deps=(), inc=False, hard=()):
        if Emit.SERIAL or getattr(self, 'serial_now', False):
            inc = True
            deps = list(deps) + [Tok(k, self.cnt[k]) for k in self.cnt]
            self._waits(eng, deps, dma=True)
            self.cnt[eng] += 1
            self.ops[eng].append(("o", fn, eng))
            return Tok(eng, self.cnt[eng])
        self._waits(eng, deps)
        self._waits(eng, hard, dma=True)
        tok = None
        if inc:
            self.cnt[eng] += 1
            tok = Tok(eng, self.cnt[eng])
        self.ops[eng].append(("o", fn, eng if inc else None))
        return tok

    def dma(self, queue, fn, deps=()):
        eng = DQUEUE[queue]
        if Emit.SERIAL or getattr(self, 'serial_now', False):
            deps = list(deps) + [Tok(k, self.cnt[k]) for k in self.cnt]
        self._waits(eng, deps, dma=True)
        self.cnt[queue] += 16
        self.ops[eng].append(("o", fn, queue))
        return Tok(queue, self.cnt[queue])

    def tok(self, key):
        return Tok(key, self.cnt[key])

    def barrier(self):
        for q, eng in DQUEUE.items():
            if self.vc[eng][q] < self.cnt[q]:
                self.vc[eng][q] = self.cnt[q]
                self.ops[eng].append(("w", q, self.cnt[q]))
        self.bar_cnt += 1
        target = 5 * self.bar_cnt
        for e in ENGINES:
            self.ops[e].append(("b",))
        for e in ENGINES:
            self.ops[e].append(("w", "bar", target))
            for s in self.sem:
                self.vc[e][s] = max(self.vc[e][s], self.cnt[s])
            self.vc[e]["bar"] = target

    def finalize(self):
        with self.nc.Block() as block:
            for e in ENGINES:
                oplist = self.ops[e]
                if not oplist:
                    continue
                def runner(engobj, oplist=oplist):
                    for item in oplist:
                        if item[0] == "w":
                            sem = self.bar if item[1] == "bar" else self.sem[item[1]]
                            engobj.wait_ge(sem, item[2])
                        elif item[0] == "b":
                            engobj.drain().then_inc(self.bar, 1)
                        else:
                            ins = item[1](engobj)
                            if item[2] is not None:
                                if item[2] in DQUEUE:
                                    ins.then_inc(self.sem[item[2]], 16)
                                else:
                                    ins.then_inc(self.sem[item[2]], 1)
                getattr(block, e)(runner)


class G:
    def __init__(self, nc, fused):
        self.nc = nc
        self.fused = fused
        self.h = {}
        self.kinds = {}

    def _get(self, name, shape, dtype, kind):
        if name in self.h:
            return self.h[name]
        t = self.nc.dram_tensor(name, list(shape), dtype, kind=kind).ap()
        self.h[name] = t
        self.kinds[name] = kind
        return t

    def inp(self, name, shape, dtype):
        return self._get(name, shape, dtype, "ExternalInput")

    def out(self, name, shape, dtype):
        return self._get(name, shape, dtype, "ExternalOutput")

    def sread(self, name, shape, dtype):
        return self._get(name, shape, dtype,
                         "Internal" if self.fused else "ExternalInput")

    def swrite(self, name, shape, dtype):
        return self._get(name, shape, dtype,
                         "Internal" if self.fused else "ExternalOutput")

    def scratch(self, name, shape, dtype):
        return self._get(name, shape, dtype, "Internal")



def _blockdiag(W, transpose):
    out = np.zeros((16, 128, 128), dtype=np.float32)
    for c in range(16):
        for j in range(32):
            h = 32 * c + j
            blk = W[h] if transpose else W[h].T
            out[c, 4 * j:4 * j + 4, 4 * j:4 * j + 4] = blk
    return out


def _cols(v, n):
    return np.ascontiguousarray(v.reshape(n, 128).T)


def _rm_blk(rm):
    out = np.zeros((4, 8, 128, 128), dtype=np.float32)
    for h in range(4):
        for c in range(8):
            for j in range(8):
                out[h, c, 16 * j:16 * j + 16, 16 * j:16 * j + 16] = rm[h, 8 * c + j].T
    return out


def _rup_blk(rup):
    out = np.zeros((4, 8, 4, 128, 128), dtype=np.float32)
    for h in range(4):
        for c in range(8):
            for js in range(4):
                for j in range(8):
                    blk = rup[h, 8 * c + j, 16 * js:16 * js + 16, :].T
                    out[h, c, js, 16 * j:16 * j + 16, 16 * j:16 * j + 16] = blk
    return out


def _cproj_perm():
    perm = np.empty(GH, dtype=np.int64)
    i = 0
    for h in range(4):
        for c in range(8):
            for js in range(4):
                for j in range(8):
                    for kk in range(16):
                        perm[i] = h * 4096 + (8 * c + j) * 64 + 16 * js + kk
                        i += 1
    return perm


def _masks():
    m = np.zeros((5, 128, 512), dtype=np.float32)
    for mm in range(4):
        t = np.arange(128)[:, None] + 128 * mm
        s = np.arange(512)[None, :]
        m[mm] = (t <= s).astype(np.float32)
    m[4] = 1.0
    return m


def host_prepare(inputs):
    c = {}
    c["wup"] = np.ascontiguousarray(inputs["L0_attn_W_up"], dtype=np.float32)
    c["bq"] = _blockdiag(np.asarray(inputs["L0_attn_Wq"]), False)
    c["bk"] = _blockdiag(np.asarray(inputs["L0_attn_Wk"]), False)
    c["bv"] = _blockdiag(np.asarray(inputs["L0_attn_Wv"]), False)
    c["bqT"] = _blockdiag(np.asarray(inputs["L0_attn_Wq"]), True)
    c["bkT"] = _blockdiag(np.asarray(inputs["L0_attn_Wk"]), True)
    c["bvT"] = _blockdiag(np.asarray(inputs["L0_attn_Wv"]), True)
    c["wig"] = np.ascontiguousarray(inputs["L0_attn_W_ig"], dtype=np.float32)
    c["wfg"] = np.ascontiguousarray(inputs["L0_attn_W_fg"], dtype=np.float32)
    c["biasig"] = np.asarray(inputs["L0_attn_b_ig"]).reshape(4, 1).astype(np.float32)
    c["biasfg"] = np.asarray(inputs["L0_attn_b_fg"]).reshape(4, 1).astype(np.float32)
    cw = np.zeros((128, 64), dtype=np.float32)
    for cc in range(16):
        for i in range(4):
            cw[:, 4 * cc + i] = inputs["L0_attn_conv_w"][i, 128 * cc:128 * cc + 128]
    c["convw"] = cw
    c["convb"] = _cols(np.asarray(inputs["L0_attn_conv_b"]), 16)
    c["nw"] = _cols(np.asarray(inputs["L0_attn_norm_w"]), 16)
    c["sk"] = _cols(np.asarray(inputs["L0_attn_skip"]), 16)
    c["wdown"] = np.ascontiguousarray(inputs["L0_attn_W_down"], dtype=np.float32)
    c["wgT"] = np.ascontiguousarray(np.asarray(inputs["L0_ffn_w_gate"]).T)
    c["wuT"] = np.ascontiguousarray(np.asarray(inputs["L0_ffn_w_up"]).T)
    c["wdT"] = np.ascontiguousarray(np.asarray(inputs["L0_ffn_w_down"]).T)
    c["iwT"] = np.ascontiguousarray(np.asarray(inputs["L1_attn_input_weights"]).T)
    c["projT"] = np.ascontiguousarray(np.asarray(inputs["L1_attn_proj_matrix"]).T)
    c["xisrow"] = np.ascontiguousarray(np.asarray(inputs["L1_attn_xis"]).reshape(1, R))
    c["xiscols"] = _cols(np.asarray(inputs["L1_attn_xis"]), 8)
    c["rmblk"] = _rm_blk(np.asarray(inputs["L1_ffn_rail_maker"]))
    c["rupblk"] = _rup_blk(np.asarray(inputs["L1_ffn_rail_up_proj"]))
    c["cpT"] = np.ascontiguousarray(np.asarray(inputs["L1_ffn_w_cproj"]).T[_cproj_perm()])
    c["ident"] = np.eye(128, dtype=np.float32)
    c["masks"] = _masks()
    return c


CONST_SHAPES = {
    "wup": ((D, 4096), F32), "bq": ((16, 128, 128), F32), "bk": ((16, 128, 128), F32),
    "bv": ((16, 128, 128), F32), "bqT": ((16, 128, 128), F32), "bkT": ((16, 128, 128), F32),
    "bvT": ((16, 128, 128), F32), "wig": ((6144, 4), F32), "wfg": ((6144, 4), F32),
    "biasig": ((4, 1), F32), "biasfg": ((4, 1), F32), "convw": ((128, 64), F32), "convb": ((128, 16), F32),
    "nw": ((128, 16), F32), "sk": ((128, 16), F32), "wdown": ((INNER, D), F32),
    "wgT": ((D, HID), I32), "wuT": ((D, HID), I32), "wdT": ((HID, D), I32),
    "iwT": ((D, D), F32), "projT": ((D, D), F32), "xisrow": ((1, R), F32),
    "xiscols": ((128, 8), F32), "rmblk": ((4, 8, 128, 128), F32),
    "rupblk": ((4, 8, 4, 128, 128), F32), "cpT": ((GH, D), F32),
    "ident": ((128, 128), F32), "masks": ((5, 128, 512), F32),
}


def cinp(g, name):
    shape, dtype = CONST_SHAPES[name]
    return g.inp(name, shape, dtype)



def phase_A1(em, g, es):
    nc = em.nc
    xT = g.inp("xT", (D, S), F32)
    wup = cinp(g, "wup")
    bq, bk, bv = cinp(g, "bq"), cinp(g, "bk"), cinp(g, "bv")
    bqT, bkT, bvT = cinp(g, "bqT"), cinp(g, "bkT"), cinp(g, "bvT")
    wig, wfg = cinp(g, "wig"), cinp(g, "wfg")
    convw, convb = cinp(g, "convw"), cinp(g, "convb")

    xact_sp = g.swrite("xact", (INNER, S), F32)
    q_sp = g.swrite("qfm", (INNER, S), BF16)
    k_sp = g.swrite("kfm", (INNER, S), BF16)
    v_sp = g.swrite("vtm", (S, INNER), BF16)
    gates_sp = g.swrite("gates8", (8, S), F32)

    sb = lambda nm, shape, dt: es.enter_context(nc.sbuf_tensor(nm, shape, dt))
    ps = lambda nm, shape: es.enter_context(nc.psum_tensor(nm, shape, F32))

    wup_bf = sb("a1_wupbf", [128, 8 * INNER], BF16)
    stage32 = sb("a1_stage", [128, 2 * INNER], F32)
    blkq = sb("a1_blkq", [128, 2048], BF16)
    blkk = sb("a1_blkk", [128, 2048], BF16)
    blkv = sb("a1_blkv", [128, 2048], BF16)
    wigfg = sb("a1_wigfg", [128, 48 * 8], F32)
    wfold_a = sb("a1_wfa", [128, 16 * 8], F32)
    wfold_v = sb("a1_wfv", [128, 16 * 8], F32)
    cw = sb("a1_cw", [128, 64], F32)
    cb = sb("a1_cb", [128, 16], F32)
    convbuf = sb("a1_conv", [128, 16 * 515], F32)
    xct = sb("a1_xct", [128, TILE], F32)
    xe = sb("a1_xe", [128, 8 * TILE], F32)
    xb = sb("a1_xb", [128, 8 * TILE], BF16)
    xa32 = sb("a1_xa32", [128, 2 * TILE], F32)
    xab = sb("a1_xab", [128, 2 * TILE], BF16)
    xmb = sb("a1_xmb", [128, 2 * TILE], BF16)
    qst = sb("a1_qst", [128, 16 * TILE], BF16)
    kst = sb("a1_kst", [128, 16 * TILE], BF16)
    vst = sb("a1_vst", [128, 4 * INNER], BF16)
    gac = sb("a1_gac", [8, S], F32)

    pW = ps("a1_pW", [128, 1024])
    pQK = ps("a1_pQK", [128, 1024])
    pV = ps("a1_pV", [128, 1024])
    pG = ps("a1_pG", [8, TILE])

    wup3 = wup.rearrange("(c p) n -> c p n", p=128)
    t_wup = [None] * 8
    for ic in range(8):
        ld = em.dma("dsync", lambda e, ic=ic: e.dma_start(
            out=stage32[:, (ic % 2) * INNER:(ic % 2 + 1) * INNER],
            in_=wup3[ic, :, 0:INNER]),
            deps=[t_wup[ic - 2]] if ic >= 2 else [])
        t_wup[ic] = em.on("vector", lambda e, ic=ic: e.tensor_copy(
            wup_bf[:, ic * INNER:(ic + 1) * INNER],
            stage32[:, (ic % 2) * INNER:(ic % 2 + 1) * INNER]), deps=[ld], inc=True)
    t_cast = []
    for i, (src, dst) in enumerate(((bq, blkq), (bk, blkk), (bv, blkv))):
        ld = em.dma("dsync", lambda e, src=src, i=i: e.dma_start(
            out=stage32[:, (i % 2) * 2048:(i % 2 + 1) * 2048],
            in_=src.rearrange("c p n -> p c n")),
            deps=[t_wup[6 + (i % 2)]] + ([t_cast[i - 2]] if i >= 2 else []))
        t_cast.append(em.on("vector", lambda e, dst=dst, i=i: e.tensor_copy(
            dst[:, :], stage32[:, (i % 2) * 2048:(i % 2 + 1) * 2048]), deps=[ld], inc=True))
    wv8 = wigfg[:, :].rearrange("p (c gg) -> p c gg", gg=8)
    t_wg1 = em.dma("dsync", lambda e: e.dma_start(
        out=wv8[:, :, 0:4], in_=wig.rearrange("(c p) g -> p c g", p=128)))
    t_wg2 = em.dma("dsync", lambda e: e.dma_start(
        out=wv8[:, :, 4:8], in_=wfg.rearrange("(c p) g -> p c g", p=128)))
    t_cw = em.dma("dsync", lambda e: e.dma_start(out=cw[:, :], in_=convw))
    t_cb = em.dma("dsync", lambda e: e.dma_start(out=cb[:, :], in_=convb))
    ldq = em.dma("dsync", lambda e: e.dma_start(
        out=stage32[:, 0:2048], in_=bqT.rearrange("c p n -> p c n")),
        deps=[t_cast[2]])
    mm = None
    for c in range(16):
        mm = em.on("tensor", lambda e, c=c: e.matmul(
            pV[:, 8 * c:8 * c + 8], stage32[:, c * 128:(c + 1) * 128],
            wigfg[:, c * 8:(c + 1) * 8], start=(c == 0), stop=False),
            deps=[ldq, t_wg1, t_wg2] if c == 0 else [], inc=(c == 15))
    ldk = em.dma("dsync", lambda e: e.dma_start(
        out=stage32[:, 0:2048], in_=bkT.rearrange("c p n -> p c n")), deps=[mm])
    for c in range(16):
        mm = em.on("tensor", lambda e, c=c: e.matmul(
            pV[:, 8 * c:8 * c + 8], stage32[:, c * 128:(c + 1) * 128],
            wigfg[:, (16 + c) * 8:(17 + c) * 8], start=False, stop=True),
            deps=[ldk] if c == 0 else [], inc=(c == 15))
    t_folda = em.on("vector", lambda e: e.tensor_copy(
        wfold_a[:, :], pV[:, 0:128]), deps=[mm], inc=True)
    ldv = em.dma("dsync", lambda e: e.dma_start(
        out=stage32[:, 0:2048], in_=bvT.rearrange("c p n -> p c n")), deps=[mm])
    for c in range(16):
        mm = em.on("tensor", lambda e, c=c: e.matmul(
            pV[:, 512 + 8 * c:512 + 8 * c + 8], stage32[:, c * 128:(c + 1) * 128],
            wigfg[:, (32 + c) * 8:(33 + c) * 8], start=(c == 0), stop=True),
            deps=[ldv] if c == 0 else [], inc=(c == 15))
    t_foldv = em.on("vector", lambda e: e.tensor_copy(
        wfold_v[:, :], pV[:, 512:640]), deps=[mm], inc=True)
    t_zero = em.on("vector", lambda e: e.memset(convbuf[:, :], 0.0), inc=True)

    cb3 = convbuf[:, :].rearrange("p (c n) -> p c n", n=515)
    xT3 = xT.rearrange("(c p) s -> c p s", p=128)
    q3 = q_sp.rearrange("(c p) s -> c p s", p=128)
    k3 = k_sp.rearrange("(c p) s -> c p s", p=128)
    xa3 = xact_sp.rearrange("(c p) s -> c p s", p=128)
    v3 = v_sp.rearrange("(sc p) d -> sc p d", p=128)

    tx_b = None
    t_evW = [None, None]
    t_silu = [None] * 16
    t_slb = [None] * 16
    t_store_xa = [None] * 16
    t_mmG = [None] * 16
    t_mmq = [None] * 16
    t_mmk = [None] * 16
    t_mmv = [None] * 16
    t_evv_slot = [t_folda, t_foldv]
    war_q = war_k = None
    t_spv = [None] * 4
    t_gacev = None
    for t in range(NT):
        ld_x = em.dma("dsync", lambda e, t=t: e.dma_start(
            out=xe[:, :].rearrange("p (c s) -> p c s", s=TILE),
            in_=xT3[:, :, t * TILE:(t + 1) * TILE].rearrange("c p s -> p c s")),
            deps=[tx_b])
        tx_b = em.on("vector", lambda e: e.tensor_copy(xb[:, :], xe[:, :]),
                     deps=[ld_x, t_wup[-1], em.tok("tensor")], inc=True)
        th = em.on("vector", lambda e: e.tensor_copy(
            cb3[:, :, 0:3], cb3[:, :, 512:515]), deps=[t_zero], inc=True)
        vcount = 0
        for oc in range(16):
            so = oc % 2
            mmW = None
            for ic in range(8):
                mmW = em.on("tensor", lambda e, oc=oc, ic=ic, so=so: e.matmul(
                    pW[:, so * 512:so * 512 + TILE],
                    wup_bf[:, ic * INNER + oc * 128:ic * INNER + (oc + 1) * 128],
                    xb[:, ic * TILE:(ic + 1) * TILE],
                    start=(ic == 0), stop=(ic == 7)),
                    deps=([tx_b, t_evW[so]] if ic == 0 else []), inc=(ic == 7))
            ev = em.on("vector", lambda e, oc=oc, so=so: e.tensor_copy(
                cb3[:, oc, 3:515], pW[:, so * 512:so * 512 + TILE]),
                deps=[mmW, th, t_mmG[oc]], inc=True)
            t_evW[so] = ev
            em.on("vector", lambda e, oc=oc: e.tensor_scalar(
                xct[:, :], cb3[:, oc, 0:512], cw[:, 4 * oc:4 * oc + 1], None, ALU.mult),
                deps=[t_cw, t_silu[oc - 1], t_slb[oc - 1]])
            for i in range(1, 3):
                em.on("vector", lambda e, oc=oc, i=i: e.scalar_tensor_tensor(
                    xct[:, :], cb3[:, oc, i:i + 512], cw[:, 4 * oc + i:4 * oc + i + 1],
                    xct[:, :], ALU.mult, ALU.add))
            last_conv = em.on("vector", lambda e, oc=oc: e.scalar_tensor_tensor(
                xct[:, :], cb3[:, oc, 3:515], cw[:, 4 * oc + 3:4 * oc + 4],
                xct[:, :], ALU.mult, ALU.add), inc=True)
            t_silu[oc] = em.on("scalar", lambda e, oc=oc, so=so: e.activation(
                xa32[:, so * TILE:(so + 1) * TILE], xct[:, :], AF.Silu,
                bias=cb[:, oc:oc + 1], scale=1.0),
                deps=[last_conv, t_cb, t_store_xa[oc - 2], t_mmG[oc - 2]], inc=True)
            t_slb[oc] = em.on("scalar", lambda e, oc=oc, so=so: e.activation(
                xab[:, so * TILE:(so + 1) * TILE], xct[:, :], AF.Silu,
                bias=cb[:, oc:oc + 1], scale=1.0),
                deps=[last_conv, t_mmq[oc - 2], t_mmk[oc - 2]], inc=True)
            t_store_xa[oc] = em.dma("dscalar", lambda e, oc=oc, t=t, so=so: e.dma_start(
                out=xa3[oc, :, t * TILE:(t + 1) * TILE],
                in_=xa32[:, so * TILE:(so + 1) * TILE]), deps=[t_silu[oc]])
            txm = em.on("vector", lambda e, oc=oc, so=so: e.tensor_copy(
                xmb[:, so * TILE:(so + 1) * TILE], cb3[:, oc, 3:515]),
                deps=[ev, t_mmv[oc - 2]], inc=True)
            t_mmq[oc] = em.on("tensor", lambda e, oc=oc, so=so: e.matmul(
                pQK[:, 0:TILE], blkq[:, oc * 128:(oc + 1) * 128],
                xab[:, so * TILE:(so + 1) * TILE], start=True, stop=True),
                deps=[t_slb[oc], t_cast[0], war_q], inc=True)
            war_q = em.on("vector", lambda e, oc=oc: e.tensor_copy(
                qst[:, oc * TILE:(oc + 1) * TILE], pQK[:, 0:TILE]),
                deps=[t_mmq[oc]], inc=True)
            t_mmk[oc] = em.on("tensor", lambda e, oc=oc, so=so: e.matmul(
                pQK[:, 512:512 + TILE], blkk[:, oc * 128:(oc + 1) * 128],
                xab[:, so * TILE:(so + 1) * TILE], start=True, stop=True),
                deps=[t_cast[1], war_k], inc=True)
            war_k = em.on("vector", lambda e, oc=oc: e.tensor_copy(
                kst[:, oc * TILE:(oc + 1) * TILE], pQK[:, 512:512 + TILE]),
                deps=[t_mmk[oc]], inc=True)
            em.on("tensor", lambda e, oc=oc, so=so: e.matmul(
                pG[:, :], wfold_a[:, oc * 8:(oc + 1) * 8],
                xa32[:, so * TILE:(so + 1) * TILE],
                start=(oc == 0), stop=False),
                deps=[t_silu[oc], t_folda, t_foldv] + ([t_gacev] if oc == 0 else []))
            t_mmG[oc] = em.on("tensor", lambda e, oc=oc: e.matmul(
                pG[:, :], wfold_v[:, oc * 8:(oc + 1) * 8], cb3[:, oc, 3:515],
                start=False, stop=(oc == 15)), deps=[], inc=True)
            for sc in range(4):
                sv = vcount % 2
                mmv = em.on("tensor", lambda e, oc=oc, sc=sc, sv=sv, so=so: e.matmul(
                    pV[:, sv * 512:sv * 512 + 128],
                    xmb[:, so * TILE + sc * 128:so * TILE + (sc + 1) * 128],
                    blkv[:, oc * 128:(oc + 1) * 128], start=True, stop=True),
                    deps=[txm, t_cast[2], t_evv_slot[sv],
                          t_spv[sc] if oc == 0 else None], inc=True)
                t_mmv[oc] = mmv
                t_evv_slot[sv] = em.on("vector", lambda e, oc=oc, sc=sc, sv=sv: e.tensor_copy(
                    vst[:, sc * INNER + oc * 128:sc * INNER + (oc + 1) * 128],
                    pV[:, sv * 512:sv * 512 + 128]), deps=[mmv], inc=True)
                vcount += 1
        t_gacev = em.on("vector", lambda e, t=t: e.tensor_copy(
            gac[:, t * TILE:(t + 1) * TILE], pG[:, :]), deps=[t_mmG[15]], inc=True)
        lastv = em.tok("vector")
        spq = em.dma("dscalar", lambda e, t=t: e.dma_start(
            out=q3[:, :, t * TILE:(t + 1) * TILE].rearrange("c p s -> p c s"),
            in_=qst[:, :].rearrange("p (c s) -> p c s", s=TILE)), deps=[war_q])
        spk = em.dma("dscalar", lambda e, t=t: e.dma_start(
            out=k3[:, :, t * TILE:(t + 1) * TILE].rearrange("c p s -> p c s"),
            in_=kst[:, :].rearrange("p (c s) -> p c s", s=TILE)), deps=[war_k])
        for sc in range(4):
            t_spv[sc] = em.dma("dscalar", lambda e, t=t, sc=sc: e.dma_start(
                out=v3[4 * t + sc, :, :], in_=vst[:, sc * INNER:(sc + 1) * INNER]),
                deps=[lastv])
        war_q, war_k = spq, spk
    em.dma("dgp", lambda e: e.dma_start(out=gates_sp, in_=gac[:, :]),
           deps=[t_gacev])



def phase_A2(em, g, es):
    nc = em.nc
    xT = g.inp("xT", (D, S), F32)
    wup = cinp(g, "wup")
    sz_sp = g.swrite("sz", (INNER, S), F32)

    sb = lambda nm, shape, dt: es.enter_context(nc.sbuf_tensor(nm, shape, dt))
    wup_bf = sb("a2_wupbf", [128, 8 * INNER], BF16)
    stage32 = sb("a2_stage", [128, 2 * INNER], F32)
    xe = sb("a2_xe", [128, 8 * TILE], F32)
    xb = sb("a2_xb", [128, 8 * TILE], BF16)
    szs = sb("a2_szs", [128, 2 * TILE], F32)
    pW = es.enter_context(nc.psum_tensor("a2_pW", [128, 1024], F32))

    wup3 = wup.rearrange("(c p) n -> c p n", p=128)
    t_wup = [None] * 8
    for ic in range(8):
        ld = em.dma("dsync", lambda e, ic=ic: e.dma_start(
            out=stage32[:, (ic % 2) * INNER:(ic % 2 + 1) * INNER],
            in_=wup3[ic, :, INNER:2 * INNER]),
            deps=[t_wup[ic - 2]] if ic >= 2 else [])
        t_wup[ic] = em.on("vector", lambda e, ic=ic: e.tensor_copy(
            wup_bf[:, ic * INNER:(ic + 1) * INNER],
            stage32[:, (ic % 2) * INNER:(ic % 2 + 1) * INNER]), deps=[ld], inc=True)

    xT3 = xT.rearrange("(c p) s -> c p s", p=128)
    sz3 = sz_sp.rearrange("(c p) s -> c p s", p=128)
    tx_b = None
    t_sl = [None] * 16
    t_st = [None] * 16
    for t in range(NT):
        ld_x = em.dma("dsync", lambda e, t=t: e.dma_start(
            out=xe[:, :].rearrange("p (c s) -> p c s", s=TILE),
            in_=xT3[:, :, t * TILE:(t + 1) * TILE].rearrange("c p s -> p c s")),
            deps=[tx_b])
        tx_b = em.on("vector", lambda e: e.tensor_copy(xb[:, :], xe[:, :]),
                     deps=[ld_x, t_wup[-1], em.tok("tensor")], inc=True)
        for oc in range(16):
            so = oc % 2
            mmW = None
            for ic in range(8):
                mmW = em.on("tensor", lambda e, oc=oc, ic=ic, so=so: e.matmul(
                    pW[:, so * 512:so * 512 + TILE],
                    wup_bf[:, ic * INNER + oc * 128:ic * INNER + (oc + 1) * 128],
                    xb[:, ic * TILE:(ic + 1) * TILE],
                    start=(ic == 0), stop=(ic == 7)),
                    deps=([tx_b, t_sl[oc - 2]] if ic == 0 else []),
                    inc=(ic == 7))
            t_sl[oc] = em.on("scalar", lambda e, so=so: e.activation(
                szs[:, so * TILE:(so + 1) * TILE], pW[:, so * 512:so * 512 + TILE],
                AF.Silu), deps=[mmW, t_st[oc - 2]], inc=True)
            t_st[oc] = em.dma("dscalar", lambda e, oc=oc, t=t, so=so: e.dma_start(
                out=sz3[oc, :, t * TILE:(t + 1) * TILE],
                in_=szs[:, so * TILE:(so + 1) * TILE]), deps=[t_sl[oc]])




def phase_B(em, g, es):
    nc = em.nc
    q_sp = g.sread("qfm", (INNER, S), BF16)
    k_sp = g.sread("kfm", (INNER, S), BF16)
    v_sp = g.sread("vtm", (S, INNER), BF16)
    gates_sp = g.sread("gates8", (8, S), F32)
    xact_sp = g.sread("xact", (INNER, S), F32)
    sz_sp = g.sread("sz", (INNER, S), F32)
    biasig, biasfg = cinp(g, "biasig"), cinp(g, "biasfg")
    nw_c, sk_c = cinp(g, "nw"), cinp(g, "sk")
    ident = cinp(g, "ident")
    masks = cinp(g, "masks")
    hst_sp = g.swrite("hst", (INNER, S), BF16)
    grows = (g.swrite if DEBUG_B else g.scratch)("grows", (1, 8 * S), F32)
    if DEBUG_B:
        dbg = g.swrite("bdbg", (128, 1024 + 128 + 32 + 32), F32)
        dbg2 = g.swrite("bdbg2", (128, 2048 + 4096), F32)

    sb = lambda nm, shape, dt: es.enter_context(nc.sbuf_tensor(nm, shape, dt))
    H = S // 2
    igb = sb("b_igb", [4, H], F32)
    fgb = sb("b_fgb", [4, H], F32)
    r4a = sb("b_r4a", [4, H], F32)
    r4b = sb("b_r4b", [4, H], F32)
    ones4 = sb("b_ones4", [4, H], F32)
    Mrow = sb("b_Mrow", [4, H], F32)
    fcar = sb("b_fcar", [4, 1], F32)
    mcar = sb("b_mcar", [4, 1], F32)
    bgi = sb("b_bgi", [4, 1], F32)
    bgf = sb("b_bgf", [4, 1], F32)
    idsb = sb("b_id", [4, 4], F32)
    crow = sb("b_crow", [1, 32], F32)
    negc = sb("b_negc", [1, 32], F32)
    onesrow = sb("b_onesrow", [1, 128], F32)
    onescb = sb("b_onescb", [128, 1], BF16)
    onesc32 = sb("b_onesc32", [128, 1], F32)
    negbc = sb("b_negbc", [128, 32], F32)
    acols = sb("b_acols", [128, 128], F32)
    cexp = sb("b_cexp", [128, 1024], F32)
    msks = sb("b_msks", [128, 5 * 512], F32)
    nwsb = sb("b_nw", [128, 16], F32)
    sksb = sb("b_sk", [128, 16], F32)
    Mseg = sb("b_Mseg", [1, 512], F32)
    emdseg = sb("b_emdseg", [1, 512], F32)
    kb = sb("b_kb", [128, 4 * S], BF16)
    vb = sb("b_vb", [128, 32 * 512], BF16)
    qb = sb("b_qb", [128, 4 * 512], BF16)
    xasl = sb("b_xasl", [128, 4 * 512], F32)
    szsl = sb("b_szsl", [128, 4 * 512], F32)
    pbf = sb("b_pbf", [128, 4 * 512], BF16)
    hbuf = sb("b_hbuf", [128, 4 * 512], F32)
    tb2 = sb("b_tb2", [128, 512], F32)
    rows = sb("b_rows", [1, 8 * 512], F32)
    tbuf = sb("b_tbuf", [128, 512], F32)
    hsb = sb("b_hsb", [128, 4 * 512], BF16)

    pP = es.enter_context(nc.psum_tensor("b_pP", [128, 1024], F32))
    pH = es.enter_context(nc.psum_tensor("b_pH", [128, 2048], F32))
    pR = es.enter_context(nc.psum_tensor("b_pR", [1, 512], F32))

    ld3 = em.dma("dgp", lambda e: e.dma_start(out=bgi[:, :], in_=biasig))
    ld4 = em.dma("dgp", lambda e: e.dma_start(out=bgf[:, :], in_=biasfg))
    ld5 = em.dma("dgp", lambda e: e.dma_start(out=idsb[:, :], in_=ident[0:4, 0:4]))
    ld6 = em.dma("dsync", lambda e: e.dma_start(
        out=msks[:, :].rearrange("p (m s) -> p m s", s=512), in_=masks.rearrange("m p s -> p m s")))
    ld7 = em.dma("dsync", lambda e: e.dma_start(out=nwsb[:, :], in_=nw_c))
    ld8 = em.dma("dsync", lambda e: e.dma_start(out=sksb[:, :], in_=sk_c))
    em.on("vector", lambda e: e.memset(ones4[:, :], 1.0))
    t_on1 = em.on("vector", lambda e: e.memset(onesrow[:, :], 1.0), inc=True)
    em.on("vector", lambda e: e.memset(onescb[:, :], 1.0))
    t_on2 = em.on("vector", lambda e: e.memset(onesc32[:, :], 1.0), inc=True)
    gr8 = grows.rearrange("o (h s) -> (o h) s", s=S)
    ttr = None
    s1 = s2 = None
    tfc = tmc = None
    for ch in range(2):
        sl0 = ch * H
        ld1 = em.dma("dgp", lambda e, sl0=sl0: e.dma_start(
            out=igb[:, :], in_=gates_sp[0:4, sl0:sl0 + H]),
            deps=[ttr])
        ld2 = em.dma("dgp", lambda e, sl0=sl0: e.dma_start(
            out=fgb[:, :], in_=gates_sp[4:8, sl0:sl0 + H]), deps=[ttr])
        t_ig = em.on("vector", lambda e: e.tensor_scalar(
            igb[:, :], igb[:, :], bgi[:, 0:1], None, ALU.add), deps=[ld1, ld3], inc=True)
        t_fg = em.on("vector", lambda e: e.tensor_scalar(
            fgb[:, :], fgb[:, :], bgf[:, 0:1], None, ALU.add), deps=[ld2, ld4], inc=True)
        t1 = em.on("scalar", lambda e: e.activation(r4a[:, :], fgb[:, :], AF.Abs),
                   deps=[t_fg, s1], inc=True)
        t2 = em.on("scalar", lambda e: e.activation(r4a[:, :], r4a[:, :], AF.Exp, scale=-1.0),
                   deps=[t1], inc=True)
        t3 = em.on("scalar", lambda e: e.activation(r4a[:, :], r4a[:, :], AF.Ln, bias=1.0),
                   deps=[t2], inc=True)
        t4 = em.on("vector", lambda e: e.tensor_scalar_min(r4b[:, :], fgb[:, :], 0.0),
                   deps=[t_fg, s2], inc=True)
        t5 = em.on("vector", lambda e: e.tensor_sub(r4b[:, :], r4b[:, :], r4a[:, :]),
                   deps=[t3, t4], inc=True)
        t6 = em.on("vector", lambda e: e.tensor_tensor_scan(
            r4a[:, :], ones4[:, :], r4b[:, :], 0.0, ALU.mult, ALU.add),
            deps=[t5], inc=True)
        if ch == 1:
            t6 = em.on("vector", lambda e: e.tensor_scalar(
                r4a[:, :], r4a[:, :], fcar[:, 0:1], None, ALU.add),
                deps=[t6], hard=[tfc], inc=True)
        tfc = em.on("vector", lambda e: e.tensor_copy(fcar[:, :], r4a[:, H - 1:H]),
                    deps=[t6], hard=[t6], inc=True)
        t7 = em.on("vector", lambda e: e.tensor_sub(r4b[:, :], igb[:, :], r4a[:, :]),
                   deps=[t6, t_ig], inc=True)
        t8 = em.on("vector", lambda e: e.tensor_tensor_scan(
            Mrow[:, :], ones4[:, :], r4b[:, :], -1e30, ALU.mult, ALU.max),
            deps=[t7, s1], inc=True)
        if ch == 1:
            t8 = em.on("vector", lambda e: e.tensor_scalar(
                Mrow[:, :], Mrow[:, :], mcar[:, 0:1], None, ALU.max),
                deps=[t8], hard=[tmc], inc=True)
        tmc = em.on("vector", lambda e: e.tensor_copy(mcar[:, :], Mrow[:, H - 1:H]),
                    deps=[t8], hard=[t8], inc=True)
        for jc in range(16):
            ttr = em.on("tensor", lambda e, jc=jc, ch=ch: e.transpose(
                pP[:, 4 * (16 * ch + jc):4 * (16 * ch + jc) + 4],
                r4b[:, 128 * jc:128 * (jc + 1)], idsb[:, :]),
                deps=[t7, ld5] if jc == 0 else [], inc=(jc == 15))
        s1 = em.dma("dgp", lambda e, sl0=sl0: e.dma_start(
            out=gr8[0:4, sl0:sl0 + H], in_=Mrow[:, :]), deps=[t8])
        t9 = em.on("vector", lambda e: e.tensor_add(r4b[:, :], r4a[:, :], Mrow[:, :]),
                   deps=[t8, ttr], inc=True)
        t10 = em.on("scalar", lambda e: e.activation(r4b[:, :], r4b[:, :], AF.Exp, scale=-1.0),
                    deps=[t9], inc=True)
        s2 = em.dma("dgp", lambda e, sl0=sl0: e.dma_start(
            out=gr8[4:8, sl0:sl0 + H], in_=r4b[:, :]), deps=[t10])
    t_ac = em.on("vector", lambda e: e.tensor_copy(acols[:, :], pP[:, 0:128]),
                 deps=[ttr], inc=True)
    ld_cr = em.dma("dgp", lambda e: e.dma_start(
        out=crow[0:1, :].rearrange("o (c u) -> o c u", u=1),
        in_=grows[0:1, 0:4 * S].rearrange("o (c k) -> o c k", k=512)[:, :, 511:512],
        allow_slow_non_contiguous=True), deps=[s1])
    t_nc = em.on("vector", lambda e: e.tensor_scalar(
        negc[:, :], crow[:, :], -1.0, -LAM, ALU.mult, ALU.add), deps=[ld_cr], inc=True)
    mmb = em.on("tensor", lambda e: e.matmul(
        pP[:, 512:544], onesrow[:, :], negc[:, :], start=True, stop=True),
        deps=[t_nc, t_on1, t_ac], inc=True)
    t_nb = em.on("vector", lambda e: e.tensor_copy(negbc[:, :], pP[:, 512:544]),
                 deps=[mmb], inc=True)
    if DEBUG_B:
        em.dma("dscalar", lambda e: e.dma_start(out=dbg[:, 1024:1152], in_=acols[:, :]), deps=[t_ac])
        em.dma("dscalar", lambda e: e.dma_start(out=dbg[0:1, 1152:1184], in_=crow[:, :]), deps=[ld_cr])
        em.dma("dscalar", lambda e: e.dma_start(out=dbg[:, 1184:1216], in_=negbc[:, :]), deps=[t_nb])
    t_cx = [None] * 32
    for h in range(4):
        for i in range(8):
            col = 8 * h + i
            ahsl = acols[:, :].rearrange("p (jc hh) -> p hh jc", hh=4)[:, h, :]
            tsa = em.on("vector", lambda e, col=col, ahsl=ahsl: e.tensor_scalar(
                cexp[:, col * 32:(col + 1) * 32], ahsl,
                negbc[:, col:col + 1], 0.0, ALU.add, ALU.min),
                deps=[t_ac, t_nb], inc=True)
            t_cx[col] = em.on("scalar", lambda e, col=col: e.activation(
                cexp[:, col * 32:(col + 1) * 32], cexp[:, col * 32:(col + 1) * 32],
                AF.Exp), deps=[tsa], inc=True)

    if DEBUG_B:
        em.dma("dscalar", lambda e: e.dma_start(out=dbg[:, 0:1024], in_=cexp[:, :]), deps=[t_cx[31]])
    kfm3 = k_sp.rearrange("(c p) s -> c p s", p=128)
    qfm3 = q_sp.rearrange("(c p) s -> c p s", p=128)
    vtm3 = v_sp.rearrange("(j p) d -> j p d", p=128)
    xa4 = xact_sp.rearrange("(c p) s -> c p s", p=128)
    sz4 = sz_sp.rearrange("(c p) s -> c p s", p=128)
    hs4 = hst_sp.rearrange("(c p) s -> c p s", p=128)

    t_pbf = [t_ac, t_nb]
    t_rab = None
    t_rg = None
    t_hev = [None] * 4
    t_app = [None, None, None, mmb]
    t_sp = None
    for h in range(4):
        ld_k = em.dma("dsync", lambda e, h=h: e.dma_start(
            out=kb[:, :].rearrange("p (dc s) -> p dc s", s=S),
            in_=kfm3[4 * h:4 * h + 4, :, :].rearrange("c p s -> p c s")),
            deps=[em.tok("tensor")])
        ld_v = em.dma("dsync", lambda e, h=h: e.dma_start(
            out=vb[:, :].rearrange("p (j d) -> p j d", d=512),
            in_=vtm3[:, :, 512 * h:512 * (h + 1)].rearrange("j p d -> p j d")),
            deps=[em.tok("tensor")])
        for i in range(8):
            col = 8 * h + i
            ld_q = em.dma("dsync", lambda e, h=h, i=i: e.dma_start(
                out=qb[:, :].rearrange("p (dc s) -> p dc s", s=512),
                in_=qfm3[4 * h:4 * h + 4, :, 512 * i:512 * (i + 1)].rearrange("c p s -> p c s")),
                deps=[em.tok("tensor")])
            ld_xa = em.dma("dsync", lambda e, h=h, i=i: e.dma_start(
                out=xasl[:, :].rearrange("p (dc s) -> p dc s", s=512),
                in_=xa4[4 * h:4 * h + 4, :, 512 * i:512 * (i + 1)].rearrange("c p s -> p c s")),
                deps=[t_app[3]])
            ld_sz = em.dma("dsync", lambda e, h=h, i=i: e.dma_start(
                out=szsl[:, :].rearrange("p (dc s) -> p dc s", s=512),
                in_=sz4[4 * h:4 * h + 4, :, 512 * i:512 * (i + 1)].rearrange("c p s -> p c s")),
                deps=[t_app[3]])
            ld_M = em.dma("dgp", lambda e, col=col: e.dma_start(
                out=Mseg[:, :], in_=grows[0:1, col * 512:(col + 1) * 512]),
                deps=[s1, t_rg])
            ld_e = em.dma("dgp", lambda e, col=col: e.dma_start(
                out=emdseg[:, :], in_=grows[0:1, 4 * S + col * 512:4 * S + (col + 1) * 512]),
                deps=[s2, t_rg])
            nj = 4 * (i + 1)
            last_rs = None
            last_pv = None
            em.serial_now = ("j" in B_SER)
            for j in range(nj):
                sl = j % 2
                sb4 = j % 4
                mmp = None
                for dc in range(4):
                    mmp = em.on("tensor", lambda e, j=j, dc=dc, sl=sl: e.matmul(
                        pP[:, sl * 512:sl * 512 + 512],
                        kb[:, dc * S + j * 128:dc * S + (j + 1) * 128],
                        qb[:, dc * 512:(dc + 1) * 512],
                        start=(dc == 0), stop=(dc == 3)),
                        deps=([ld_k, ld_q, t_pbf[sl]] if dc == 0 else []),
                        inc=(dc == 3))
                m = (j - 4 * i) if j >= 4 * i else 4
                t_pbf[sl] = em.on("vector", lambda e, j=j, sb4=sb4, sl=sl, col=col, m=m: e.scalar_tensor_tensor(
                    pbf[:, sb4 * 512:(sb4 + 1) * 512], pP[:, sl * 512:sl * 512 + 512],
                    cexp[:, col * 32 + j:col * 32 + j + 1],
                    msks[:, m * 512:(m + 1) * 512], ALU.mult, ALU.mult),
                    deps=[mmp, t_cx[col], ld6], inc=True)
                last_rs = em.on("tensor", lambda e, j=j, sb4=sb4, nj=nj: e.matmul(
                    pR[:, :], onescb[:, 0:1], pbf[:, sb4 * 512:(sb4 + 1) * 512],
                    start=(j == 0), stop=(j == nj - 1)),
                    deps=[t_pbf[sl], t_on2] + ([t_rab] if j == 0 else []),
                    inc=(j == nj - 1))
                for dc in range(4):
                    last_pv = em.on("tensor", lambda e, j=j, dc=dc, sb4=sb4, nj=nj: e.matmul(
                        pH[:, dc * 512:(dc + 1) * 512],
                        vb[:, j * 512 + dc * 128:j * 512 + (dc + 1) * 128],
                        pbf[:, sb4 * 512:(sb4 + 1) * 512],
                        start=(j == 0), stop=(j == nj - 1)),
                        deps=([ld_v, t_app[3] if dc < 2 else t_hev[dc]] if j == 0 else []),
                        inc=(dc == 3 and j == nj - 1))
            em.serial_now = ("epi" in B_SER)
            mmu = mms = None
            for dc in range(4):
                t_hev[dc] = em.on("vector", lambda e, dc=dc: e.tensor_copy(
                    hbuf[:, dc * 512:(dc + 1) * 512], pH[:, dc * 512:(dc + 1) * 512]),
                    deps=[last_pv], inc=True)
                mmu = em.on("tensor", lambda e, dc=dc: e.matmul(
                    pP[0:1, 0:512], onesc32[:, 0:1], hbuf[:, dc * 512:(dc + 1) * 512],
                    start=(dc == 0), stop=(dc == 3)),
                    deps=[t_hev[dc]] + ([t_pbf[0]] if dc == 0 else []),
                    inc=(dc == 3))
                tsq = em.on("scalar", lambda e, dc=dc: e.activation(
                    tb2[:, :], hbuf[:, dc * 512:(dc + 1) * 512], AF.Square),
                    deps=[t_hev[dc], mms], inc=True)
                mms = em.on("tensor", lambda e, dc=dc: e.matmul(
                    pP[0:1, 512:1024], onesc32[:, 0:1], tb2[:, :],
                    start=(dc == 0), stop=(dc == 3)),
                    deps=[tsq] + ([t_pbf[1]] if dc == 0 else []),
                    inc=True)
            RO = lambda k2: rows[0:1, k2 * 512:(k2 + 1) * 512]
            r_mu = em.on("vector", lambda e: e.tensor_scalar(
                RO(0), pP[0:1, 0:512], 1.0 / 512, None, ALU.mult), deps=[mmu], inc=True)
            r_ms = em.on("vector", lambda e: e.tensor_scalar(
                RO(1), pP[0:1, 512:1024], 1.0 / 512, None, ALU.mult), deps=[mms], inc=True)
            t_pbf = [r_mu, r_ms]
            em.on("vector", lambda e: e.tensor_mul(RO(2), RO(0), RO(0)), deps=[r_mu])
            r_var = em.on("vector", lambda e: e.tensor_sub(RO(2), RO(1), RO(2)),
                          deps=[r_ms], inc=True)
            r_g = em.on("scalar", lambda e, col=col: e.activation(
                RO(3), Mseg[0:1, :], AF.Exp, scale=-1.0,
                bias=crow[0:1, col:col + 1]), deps=[ld_M, ld_cr, r_var], inc=True)
            t_rab = r_ab = em.on("scalar", lambda e: e.activation(
                RO(4), pR[0:1, :], AF.Abs), deps=[last_rs], inc=True)
            em.on("vector", lambda e: e.tensor_mul(RO(4), RO(4), RO(3)), deps=[r_g, r_ab])
            em.on("vector", lambda e: e.tensor_max(RO(4), RO(4), emdseg[0:1, :]),
                  deps=[ld_e])
            em.on("vector", lambda e: e.tensor_scalar(
                RO(4), RO(4), 1e-6, None, ALU.add))
            em.on("vector", lambda e: e.reciprocal(RO(4), RO(4)))
            r_phi = em.on("vector", lambda e: e.tensor_mul(RO(5), RO(3), RO(4)), inc=True)
            t_rg = r_phi
            em.on("vector", lambda e: e.tensor_mul(RO(6), RO(5), RO(5)))
            em.on("vector", lambda e: e.tensor_mul(RO(6), RO(6), RO(2)))
            r_u = em.on("vector", lambda e: e.tensor_scalar(
                RO(6), RO(6), 1e-6, 1e-30, ALU.add, ALU.max), inc=True)
            r_l = em.on("scalar", lambda e: e.activation(RO(6), RO(6), AF.Ln),
                        deps=[r_u], inc=True)
            r_rs = em.on("scalar", lambda e: e.activation(RO(7), RO(6), AF.Exp, scale=-0.5),
                         deps=[r_l], inc=True)
            r_sig = em.on("vector", lambda e: e.tensor_mul(RO(7), RO(7), RO(5)),
                          deps=[r_rs, r_phi], inc=True)
            mbc1 = em.on("tensor", lambda e: e.matmul(
                pH[:, 0:512], onesrow[:, :], RO(0), start=True, stop=True),
                deps=[r_mu, t_hev[3], t_on1], inc=True)
            mbc2 = em.on("tensor", lambda e: e.matmul(
                pH[:, 512:1024], onesrow[:, :], RO(7), start=True, stop=True),
                deps=[r_sig], inc=True)
            for dc in range(4):
                em.on("vector", lambda e, dc=dc: e.tensor_sub(
                    tbuf[:, :], hbuf[:, dc * 512:(dc + 1) * 512], pH[:, 0:512]),
                    deps=[mbc1])
                em.on("vector", lambda e: e.tensor_tensor(
                    tbuf[:, :], tbuf[:, :], pH[:, 512:1024], ALU.mult), deps=[mbc2])
                em.on("vector", lambda e, h=h, dc=dc: e.tensor_scalar(
                    tbuf[:, :], tbuf[:, :], nwsb[:, 4 * h + dc:4 * h + dc + 1], None,
                    ALU.mult), deps=[ld7])
                em.on("vector", lambda e, h=h, dc=dc: e.scalar_tensor_tensor(
                    tbuf[:, :], xasl[:, dc * 512:(dc + 1) * 512],
                    sksb[:, 4 * h + dc:4 * h + dc + 1], tbuf[:, :], ALU.mult, ALU.add),
                    deps=[ld_xa, ld8])
                t_app[dc] = em.on("vector", lambda e, dc=dc: e.tensor_tensor(
                    hsb[:, dc * 512:(dc + 1) * 512], tbuf[:, :],
                    szsl[:, dc * 512:(dc + 1) * 512], ALU.mult),
                    deps=[ld_sz, t_sp], inc=True)
            t_sp = em.dma("dscalar", lambda e, h=h, i=i: e.dma_start(
                out=hs4[4 * h:4 * h + 4, :, 512 * i:512 * (i + 1)].rearrange("c p s -> p c s"),
                in_=hsb[:, :].rearrange("p (dc s) -> p dc s", s=512)), deps=[t_app[3]])
            em.serial_now = False
            if DEBUG_B and (h, i) == DEBUG_B_TILE:
                em.dma("dscalar", lambda e: e.dma_start(out=dbg2[:, 0:2048], in_=hbuf[:, :]),
                       deps=[t_hev[3]])
                em.dma("dscalar", lambda e: e.dma_start(out=dbg2[0:1, 2048:2048 + 4096], in_=rows[:, :]),
                       deps=[r_sig, t_app[3]])




class LNAcc:

    def __init__(self, em, nch, tb, rows, pS, pB, onesc32, onesrow, eps, w=512):
        self.em = em
        self.w = w
        self.nch = nch
        self.tb = tb
        self.rows = rows
        self.pS = pS
        self.pB = pB
        self.onesc32 = onesc32
        self.onesrow = onesrow
        self.eps = eps
        self.k = 0
        self.mmu = None
        self.mms = None
        self.tsq = None
        self.war_mu = None
        self.war_ms = None
        self.war_b = [None, None]

    def add(self, rview, dep_r):
        em = self.em
        w = self.w
        first = (self.k == 0)
        last = (self.k == self.nch - 1)
        self.mmu = em.on("tensor", lambda e, rview=rview, w=w: e.matmul(
            self.pS[0:1, 0:w], self.onesc32[:, 0:1], rview,
            start=first, stop=last),
            deps=[dep_r] + ([self.war_mu] if first else []), inc=True)
        self.tsq = em.on("scalar", lambda e, rview=rview, w=w: e.activation(
            self.tb[:, 0:w], rview, AF.Square),
            deps=[dep_r, self.mms], inc=True)
        self.mms = em.on("tensor", lambda e, w=w: e.matmul(
            self.pS[0:1, 512:512 + w], self.onesc32[:, 0:1], self.tb[:, 0:w],
            start=first, stop=last),
            deps=[self.tsq] + ([self.war_ms] if first else []), inc=True)
        self.k += 1

    def finish(self):
        em = self.em
        w = self.w
        RO = lambda k2: self.rows[0:1, k2 * 512:k2 * 512 + w]
        n = 1.0 / (128 * self.nch)
        r_mu = em.on("vector", lambda e, n=n, w=w: e.tensor_scalar(
            RO(0), self.pS[0:1, 0:w], n, None, ALU.mult), deps=[self.mmu], inc=True)
        self.war_mu = r_mu
        r_ms = em.on("vector", lambda e, n=n, w=w: e.tensor_scalar(
            RO(1), self.pS[0:1, 512:512 + w], n, None, ALU.mult), deps=[self.mms], inc=True)
        self.war_ms = r_ms
        em.on("vector", lambda e: e.tensor_mul(RO(2), RO(0), RO(0)))
        em.on("vector", lambda e: e.tensor_sub(RO(2), RO(1), RO(2)))
        r_v = em.on("vector", lambda e: e.tensor_scalar(
            RO(2), RO(2), self.eps, 1e-30, ALU.add, ALU.max), inc=True)
        r_l = em.on("scalar", lambda e: e.activation(RO(2), RO(2), AF.Ln),
                    deps=[r_v], inc=True)
        r_rs = em.on("scalar", lambda e: e.activation(RO(3), RO(2), AF.Exp, scale=-0.5),
                     deps=[r_l], inc=True)
        mb1 = em.on("tensor", lambda e, w=w: e.matmul(
            self.pB[:, 0:w], self.onesrow[:, :], RO(0), start=True, stop=True),
            deps=[r_mu, self.war_b[0]], inc=True)
        mb2 = em.on("tensor", lambda e, w=w: e.matmul(
            self.pB[:, 512:512 + w], self.onesrow[:, :], RO(3), start=True, stop=True),
            deps=[r_rs, self.war_b[1]], inc=True)
        self.mb = (mb1, mb2)
        self.k = 0

    def apply(self, em, rview, dst32, dstbf, dep_extra=()):
        w = self.w
        t1 = em.on("vector", lambda e, rview=rview, dst32=dst32, w=w: e.tensor_sub(
            dst32, rview, self.pB[:, 0:w]), deps=[self.mb[0]] + list(dep_extra))
        t2 = em.on("vector", lambda e, dst32=dst32, w=w: e.tensor_tensor(
            dst32, dst32, self.pB[:, 512:512 + w], ALU.mult), deps=[self.mb[1]], inc=True)
        self.war_b = [t2, t2]
        if dstbf is None:
            return t2, None
        t3 = em.on("vector", lambda e, dst32=dst32, dstbf=dstbf: e.tensor_copy(
            dstbf, dst32), inc=True)
        return t2, t3


def _load_cast_weight(em, nm, src3, nchunks, cols, stage, dstbf, scale=None, lo_dst=None, after=None):
    toks = [None] * nchunks
    for k in range(nchunks):
        so = (k % 2) * cols
        ld = em.dma("dsync", lambda e, k=k, so=so: e.dma_start(
            out=stage[:, so:so + cols], in_=src3[k]),
            deps=[toks[k - 2]] if k >= 2 else [after])
        if scale is None:
            toks[k] = em.on("vector", lambda e, k=k, so=so: e.tensor_copy(
                dstbf[:, k * cols:(k + 1) * cols], stage[:, so:so + cols]),
                deps=[ld], inc=True)
            if lo_dst is not None:
                toks[k] = em.on("vector", lambda e, k=k, so=so: e.tensor_sub(
                    lo_dst[:, k * cols:(k + 1) * cols], stage[:, so:so + cols],
                    dstbf[:, k * cols:(k + 1) * cols]), deps=[toks[k]], inc=True)
        else:
            toks[k] = em.on("vector", lambda e, k=k, so=so: e.tensor_scalar(
                dstbf[:, k * cols:(k + 1) * cols], stage[:, so:so + cols],
                scale, None, ALU.mult), deps=[ld], inc=True)
    return toks



def phase_C(em, g, es):
    nc = em.nc
    xT = g.inp("xT", (D, S), F32)
    wdown = cinp(g, "wdown")
    hst_sp = g.sread("hst", (INNER, S), BF16)
    x1_sp = g.swrite("x1f", (D, S), F32)
    x1b_sp = g.swrite("x1b", (D, S), BF16)

    sb = lambda nm, shape, dt: es.enter_context(nc.sbuf_tensor(nm, shape, dt))
    wd_bf = sb("c_wd", [128, 16 * D], BF16)
    stage = sb("c_stage", [128, 2 * D], F32)
    hb = sb("c_hb", [128, 16 * TILE], BF16)
    xe = sb("c_xe", [128, 8 * TILE], F32)
    r32 = sb("c_r32", [128, 8 * TILE], F32)
    x1st = sb("c_x1", [128, 8 * TILE], F32)
    x1bs = sb("c_x1b", [128, 8 * TILE], BF16)
    tb = sb("c_tb", [128, TILE], F32)
    rows = sb("c_rows", [1, 4 * 512], F32)
    onesc32 = sb("c_on1", [128, 1], F32)
    onesrow = sb("c_on2", [1, 128], F32)
    pW = es.enter_context(nc.psum_tensor("c_pW", [128, 1024], F32))
    pS = es.enter_context(nc.psum_tensor("c_pS", [1, 1024], F32))
    pB = es.enter_context(nc.psum_tensor("c_pB", [128, 1024], F32))

    em.on("vector", lambda e: e.memset(onesc32[:, :], 1.0))
    t_on = em.on("vector", lambda e: e.memset(onesrow[:, :], 1.0), inc=True)
    wd3 = wdown.rearrange("(k p) o -> k p o", p=128)
    t_w = _load_cast_weight(em, "wd", wd3, 16, D, stage, wd_bf)
    ln = LNAcc(em, 8, tb, rows, pS, pB, onesc32, onesrow, 1e-5)
    ln.war_b = [t_on, t_on]
    hs3 = hst_sp.rearrange("(c p) s -> c p s", p=128)
    xT3 = xT.rearrange("(c p) s -> c p s", p=128)
    x13 = x1_sp.rearrange("(c p) s -> c p s", p=128)
    x1b3 = x1b_sp.rearrange("(c p) s -> c p s", p=128)
    t_evW = [None, None]
    t_radd = [None] * 8
    t_ap3 = None
    sp1 = sp2 = None
    ld_h = ld_x = None
    for t in range(NT):
        ld_h = em.dma("dsync", lambda e, t=t: e.dma_start(
            out=hb[:, :].rearrange("p (c s) -> p c s", s=TILE),
            in_=hs3[:, :, t * TILE:(t + 1) * TILE].rearrange("c p s -> p c s")),
            deps=[em.tok("tensor")])
        ld_x = em.dma("dsync", lambda e, t=t: e.dma_start(
            out=xe[:, :].rearrange("p (c s) -> p c s", s=TILE),
            in_=xT3[:, :, t * TILE:(t + 1) * TILE].rearrange("c p s -> p c s")),
            deps=[t_radd[7]])
        for oc in range(8):
            so = oc % 2
            mmW = None
            for ic in range(16):
                mmW = em.on("tensor", lambda e, oc=oc, ic=ic, so=so: e.matmul(
                    pW[:, so * 512:so * 512 + TILE],
                    wd_bf[:, ic * D + oc * 128:ic * D + (oc + 1) * 128],
                    hb[:, ic * TILE:(ic + 1) * TILE],
                    start=(ic == 0), stop=(ic == 15)),
                    deps=([ld_h, t_w[-1], t_evW[so]] if ic == 0 else []),
                    inc=(ic == 15))
            t_radd[oc] = em.on("vector", lambda e, oc=oc, so=so: e.tensor_add(
                r32[:, oc * TILE:(oc + 1) * TILE], pW[:, so * 512:so * 512 + TILE],
                xe[:, oc * TILE:(oc + 1) * TILE]), deps=[mmW, ld_x, t_ap3], inc=True)
            t_evW[so] = t_radd[oc]
            ln.add(r32[:, oc * TILE:(oc + 1) * TILE], t_radd[oc])
        ln.finish()
        for oc in range(8):
            dep_sp = [sp1, sp2] if oc < 2 else []
            t2, t3 = ln.apply(em, r32[:, oc * TILE:(oc + 1) * TILE],
                              x1st[:, oc * TILE:(oc + 1) * TILE],
                              x1bs[:, oc * TILE:(oc + 1) * TILE], dep_extra=dep_sp)
        t_ap3 = t3
        sp1 = em.dma("dscalar", lambda e, t=t: e.dma_start(
            out=x13[:, :, t * TILE:(t + 1) * TILE].rearrange("c p s -> p c s"),
            in_=x1st[:, :].rearrange("p (c s) -> p c s", s=TILE)), deps=[t3])
        sp2 = em.dma("dscalar", lambda e, t=t: e.dma_start(
            out=x1b3[:, :, t * TILE:(t + 1) * TILE].rearrange("c p s -> p c s"),
            in_=x1bs[:, :].rearrange("p (c s) -> p c s", s=TILE)), deps=[t3])



def phase_D1(em, g, es):
    nc = em.nc
    wgT, wuT = cinp(g, "wgT"), cinp(g, "wuT")
    x1b_sp = g.sread("x1b", (D, S), BF16)
    hid_sp = g.swrite("hid", (HID, S), BF16)

    sb = lambda nm, shape, dt: es.enter_context(nc.sbuf_tensor(nm, shape, dt))
    wg_bf = sb("d1_wg", [128, 8 * HID], BF16)
    wu_bf = sb("d1_wu", [128, 8 * HID], BF16)
    stage = sb("d1_stage", [128, 2 * HID], I32)
    xb = sb("d1_xb", [128, 8 * TILE], BF16)
    sg = sb("d1_sg", [128, TILE], BF16)
    hidb = sb("d1_hid", [128, 2 * 8 * TILE], BF16)
    pG = es.enter_context(nc.psum_tensor("d1_pG", [128, 1024], F32))
    pU = es.enter_context(nc.psum_tensor("d1_pU", [128, 1024], F32))

    wg3 = wgT.rearrange("(k p) o -> k p o", p=128)
    wu3 = wuT.rearrange("(k p) o -> k p o", p=128)
    t_wg = _load_cast_weight(em, "wg", wg3, 8, HID, stage, wg_bf, scale=1.0 / 127.0)
    t_wu = _load_cast_weight(em, "wu", wu3, 8, HID, stage, wu_bf, scale=1.0 / 127.0, after=t_wg[-1])
    x1b3 = x1b_sp.rearrange("(c p) s -> c p s", p=128)
    hid3 = hid_sp.rearrange("(c p) s -> c p s", p=128)
    tx = None
    t_evG = [None, None]
    t_evU = [None, None]
    t_mul = [None] * 24
    sp = [None, None, None]
    for t in range(NT):
        ld_x = em.dma("dsync", lambda e, t=t: e.dma_start(
            out=xb[:, :].rearrange("p (c s) -> p c s", s=TILE),
            in_=x1b3[:, :, t * TILE:(t + 1) * TILE].rearrange("c p s -> p c s")),
            deps=[em.tok("tensor")])
        for oc in range(24):
            so = oc % 2
            grp = (oc // 8) % 2
            mmG = mmU = None
            for ic in range(8):
                mmG = em.on("tensor", lambda e, oc=oc, ic=ic, so=so: e.matmul(
                    pG[:, so * 512:so * 512 + TILE],
                    wg_bf[:, ic * HID + oc * 128:ic * HID + (oc + 1) * 128],
                    xb[:, ic * TILE:(ic + 1) * TILE],
                    start=(ic == 0), stop=(ic == 7)),
                    deps=([ld_x, t_wg[-1], t_evG[so]] if ic == 0 else []),
                    inc=(ic == 7))
            for ic in range(8):
                mmU = em.on("tensor", lambda e, oc=oc, ic=ic, so=so: e.matmul(
                    pU[:, so * 512:so * 512 + TILE],
                    wu_bf[:, ic * HID + oc * 128:ic * HID + (oc + 1) * 128],
                    xb[:, ic * TILE:(ic + 1) * TILE],
                    start=(ic == 0), stop=(ic == 7)),
                    deps=([t_wu[-1], t_evU[so]] if ic == 0 else []),
                    inc=(ic == 7))
            t_sl = em.on("scalar", lambda e, so=so: e.activation(
                sg[:, :], pG[:, so * 512:so * 512 + TILE], AF.Silu),
                deps=[mmG, t_mul[oc - 1]], inc=True)
            t_evG[so] = t_sl
            t_mul[oc] = em.on("vector", lambda e, oc=oc, so=so, grp=grp: e.tensor_tensor(
                hidb[:, grp * 8 * TILE + (oc % 8) * TILE:grp * 8 * TILE + (oc % 8 + 1) * TILE],
                sg[:, :], pU[:, so * 512:so * 512 + TILE], ALU.mult),
                deps=[t_sl, mmU, sp[(oc // 8 + 1) % 3] if oc % 8 == 0 else None],
                inc=True)
            t_evU[so] = t_mul[oc]
            if oc % 8 == 7:
                sp[oc // 8] = em.dma("dscalar", lambda e, t=t, oc=oc, grp=grp: e.dma_start(
                    out=hid3[(oc // 8) * 8:(oc // 8) * 8 + 8, :, t * TILE:(t + 1) * TILE].rearrange("c p s -> p c s"),
                    in_=hidb[:, grp * 8 * TILE:(grp + 1) * 8 * TILE].rearrange("p (c s) -> p c s", s=TILE)),
                    deps=[t_mul[oc]])



def phase_D2(em, g, es):
    nc = em.nc
    wdT = cinp(g, "wdT")
    hid_sp = g.sread("hid", (HID, S), BF16)
    x1_sp = g.sread("x1f", (D, S), F32)
    x2_sp = g.swrite("x2f", (D, S), F32)
    x2b_sp = g.swrite("x2b", (D, S), BF16)

    sb = lambda nm, shape, dt: es.enter_context(nc.sbuf_tensor(nm, shape, dt))
    wd_bf = sb("d2_wd", [128, 24 * D], BF16)
    stage = sb("d2_stage", [128, 2 * D], I32)
    hb = sb("d2_hb", [128, 24 * TILE], BF16)
    xe = sb("d2_xe", [128, 8 * TILE], F32)
    r32 = sb("d2_r32", [128, 8 * TILE], F32)
    x2st = sb("d2_x2", [128, 8 * TILE], F32)
    x2bs = sb("d2_x2b", [128, 8 * TILE], BF16)
    tb = sb("d2_tb", [128, TILE], F32)
    rows = sb("d2_rows", [1, 4 * 512], F32)
    onesc32 = sb("d2_on1", [128, 1], F32)
    onesrow = sb("d2_on2", [1, 128], F32)
    pW = es.enter_context(nc.psum_tensor("d2_pW", [128, 1024], F32))
    pS = es.enter_context(nc.psum_tensor("d2_pS", [1, 1024], F32))
    pB = es.enter_context(nc.psum_tensor("d2_pB", [128, 1024], F32))

    em.on("vector", lambda e: e.memset(onesc32[:, :], 1.0))
    t_on = em.on("vector", lambda e: e.memset(onesrow[:, :], 1.0), inc=True)
    wd3 = wdT.rearrange("(k p) o -> k p o", p=128)
    t_w = _load_cast_weight(em, "wd2", wd3, 24, D, stage, wd_bf, scale=1.0 / 127.0)
    ln = LNAcc(em, 8, tb, rows, pS, pB, onesc32, onesrow, 1e-5)
    ln.war_b = [t_on, t_on]
    hs3 = hid_sp.rearrange("(c p) s -> c p s", p=128)
    x13 = x1_sp.rearrange("(c p) s -> c p s", p=128)
    x23 = x2_sp.rearrange("(c p) s -> c p s", p=128)
    x2b3 = x2b_sp.rearrange("(c p) s -> c p s", p=128)
    t_evW = [None, None]
    t_radd = [None] * 8
    t_ap3 = None
    sp1 = sp2 = None
    for t in range(NT):
        ld_h = em.dma("dsync", lambda e, t=t: e.dma_start(
            out=hb[:, :].rearrange("p (c s) -> p c s", s=TILE),
            in_=hs3[:, :, t * TILE:(t + 1) * TILE].rearrange("c p s -> p c s")),
            deps=[em.tok("tensor")])
        ld_x = em.dma("dsync", lambda e, t=t: e.dma_start(
            out=xe[:, :].rearrange("p (c s) -> p c s", s=TILE),
            in_=x13[:, :, t * TILE:(t + 1) * TILE].rearrange("c p s -> p c s")),
            deps=[t_radd[7]])
        for oc in range(8):
            so = oc % 2
            mmW = None
            for ic in range(24):
                mmW = em.on("tensor", lambda e, oc=oc, ic=ic, so=so: e.matmul(
                    pW[:, so * 512:so * 512 + TILE],
                    wd_bf[:, ic * D + oc * 128:ic * D + (oc + 1) * 128],
                    hb[:, ic * TILE:(ic + 1) * TILE],
                    start=(ic == 0), stop=(ic == 23)),
                    deps=([ld_h, t_w[-1], t_evW[so]] if ic == 0 else []),
                    inc=(ic == 23))
            t_radd[oc] = em.on("vector", lambda e, oc=oc, so=so: e.tensor_add(
                r32[:, oc * TILE:(oc + 1) * TILE], pW[:, so * 512:so * 512 + TILE],
                xe[:, oc * TILE:(oc + 1) * TILE]), deps=[mmW, ld_x, t_ap3], inc=True)
            t_evW[so] = t_radd[oc]
            ln.add(r32[:, oc * TILE:(oc + 1) * TILE], t_radd[oc])
        ln.finish()
        for oc in range(8):
            dep_sp = [sp1, sp2] if oc < 2 else []
            t2, t3 = ln.apply(em, r32[:, oc * TILE:(oc + 1) * TILE],
                              x2st[:, oc * TILE:(oc + 1) * TILE],
                              x2bs[:, oc * TILE:(oc + 1) * TILE], dep_extra=dep_sp)
        t_ap3 = t3
        sp1 = em.dma("dscalar", lambda e, t=t: e.dma_start(
            out=x23[:, :, t * TILE:(t + 1) * TILE].rearrange("c p s -> p c s"),
            in_=x2st[:, :].rearrange("p (c s) -> p c s", s=TILE)), deps=[t3])
        sp2 = em.dma("dscalar", lambda e, t=t: e.dma_start(
            out=x2b3[:, :, t * TILE:(t + 1) * TILE].rearrange("c p s -> p c s"),
            in_=x2bs[:, :].rearrange("p (c s) -> p c s", s=TILE)), deps=[t3])



def phase_E(em, g, es):
    nc = em.nc
    iwT, projT = cinp(g, "iwT"), cinp(g, "projT")
    xisrow, xiscols = cinp(g, "xisrow"), cinp(g, "xiscols")
    x2_sp = g.sread("x2f", (D, S), F32)
    x2b_sp = g.sread("x2b", (D, S), BF16)
    x3_sp = g.swrite("x3f", (D, S), F32)
    x3b_sp = g.swrite("x3b", (D, S), BF16)

    sb = lambda nm, shape, dt: es.enter_context(nc.sbuf_tensor(nm, shape, dt))
    pj_bf = sb("e_pj", [128, 8 * R], BF16)
    pj_lo = sb("e_pjlo", [128, 8 * R], BF16)
    iw_bf = sb("e_iw", [128, 8 * D], BF16)
    iw_lo = sb("e_iwlo", [128, 8 * D], BF16)
    stage = sb("e_stage", [128, 2 * R], F32)
    wrfs = sb("e_wr", [128, 8 * D], BF16)
    xrow = sb("e_xr", [1, R], F32)
    urow = sb("e_ur", [1, R], F32)
    xcol = sb("e_xc", [128, 8], F32)
    bcol = sb("e_bc", [128, 8], F32)
    niw = sb("e_niw", [1, R], F32)
    nx = sb("e_nx", [1, 512], F32)
    nsq = sb("e_nsq", [128, R], F32)
    xe = sb("e_xe", [128, 8 * TILE], F32)
    xb = sb("e_xb", [128, 8 * TILE], BF16)
    xlo = sb("e_xlo", [128, 8 * TILE], BF16)
    feats = sb("e_ft", [128, 8 * TILE], BF16)
    r32 = sb("e_r32", [128, 8 * TILE], F32)
    x3st = sb("e_x3", [128, 8 * TILE], F32)
    x3bs = sb("e_x3b", [128, 8 * TILE], BF16)
    tb = sb("e_tb", [128, TILE], F32)
    rows = sb("e_rows", [1, 4 * 512], F32)
    onesc32 = sb("e_on1", [128, 1], F32)
    onesrow = sb("e_on2", [1, 128], F32)
    pD = es.enter_context(nc.psum_tensor("e_pD", [128, 1024], F32))
    pW = es.enter_context(nc.psum_tensor("e_pW", [128, 1024], F32))
    pS = es.enter_context(nc.psum_tensor("e_pS", [1, 1024], F32))
    pB = es.enter_context(nc.psum_tensor("e_pB", [128, 1024], F32))

    em.on("vector", lambda e: e.memset(onesc32[:, :], 1.0))
    t_on = em.on("vector", lambda e: e.memset(onesrow[:, :], 1.0), inc=True)
    pj3 = projT.rearrange("(k p) r -> k p r", p=128)
    iw3 = iwT.rearrange("(k p) o -> k p o", p=128)
    t_pj = _load_cast_weight(em, "pj", pj3, 8, R, stage, pj_bf, lo_dst=pj_lo)
    t_iw = _load_cast_weight(em, "iw", iw3, 8, D, stage, iw_bf, lo_dst=iw_lo, after=t_pj[-1])
    ld_xr = em.dma("dgp", lambda e: e.dma_start(out=xrow[:, :], in_=xisrow))
    ld_xc = em.dma("dsync", lambda e: e.dma_start(out=xcol[:, :], in_=xiscols))
    t_ur = em.on("vector", lambda e: e.tensor_scalar(
        urow[:, :], xrow[:, :], -0.5, None, ALU.mult), deps=[ld_xr], inc=True)
    t_bc = em.on("vector", lambda e: e.tensor_scalar(
        bcol[:, :], xcol[:, :], float(np.log(1.0 / 32.0)), None, ALU.add),
        deps=[ld_xc], inc=True)
    t_sq = None
    mmn = None
    for k in range(8):
        em.on("vector", lambda e, k=k: e.scalar_tensor_tensor(
            nsq[:, :], iw_lo[:, k * D:(k + 1) * D], 2.0,
            iw_bf[:, k * D:(k + 1) * D], ALU.mult, ALU.add),
            deps=[t_iw[k], mmn])
        t_sq = em.on("vector", lambda e, k=k: e.tensor_tensor(
            nsq[:, :], nsq[:, :], iw_bf[:, k * D:(k + 1) * D], ALU.mult),
            inc=True)
        for half in range(2):
            mmn = em.on("tensor", lambda e, k=k, half=half: e.matmul(
                pS[0:1, half * 512:(half + 1) * 512], onesc32[:, 0:1],
                nsq[:, half * 512:half * 512 + 512],
                start=(k == 0), stop=(k == 7)), deps=[t_sq], inc=True)
    t_niw = em.on("vector", lambda e: e.tensor_copy(niw[0:1, 0:512], pS[0:1, 0:512]),
                  deps=[mmn], inc=True)
    t_niw = em.on("vector", lambda e: e.tensor_copy(niw[0:1, 512:1024], pS[0:1, 512:1024]),
                  deps=[mmn], inc=True)
    t_wr = None
    t_evD = [t_niw, t_niw]
    for rc in range(8):
        for ot in range(2):
            sl = (2 * rc + ot) % 2
            mmD = None
            for pp, (lhs_, rhs_) in enumerate(((pj_bf, iw_bf), (pj_bf, iw_lo), (pj_lo, iw_bf))):
                for ic in range(8):
                    mmD = em.on("tensor", lambda e, rc=rc, ot=ot, ic=ic, sl=sl, pp=pp, lhs_=lhs_, rhs_=rhs_: e.matmul(
                        pD[:, sl * 512:sl * 512 + 512],
                        lhs_[:, ic * R + rc * 128:ic * R + (rc + 1) * 128],
                        rhs_[:, ic * D + ot * 512:ic * D + ot * 512 + 512],
                        start=(pp == 0 and ic == 0), stop=False),
                        deps=([t_pj[-1], t_iw[-1], t_evD[sl]] if pp == 0 and ic == 0 else []))
            mmD = em.on("tensor", lambda e, rc=rc, ot=ot, sl=sl: e.matmul(
                pD[:, sl * 512:sl * 512 + 512], urow[0:1, rc * 128:(rc + 1) * 128],
                niw[0:1, ot * 512:(ot + 1) * 512], start=False, stop=True),
                deps=[t_ur, t_niw], inc=True)
            t_wr = em.on("scalar", lambda e, rc=rc, ot=ot, sl=sl: e.activation(
                wrfs[:, rc * D + ot * 512:rc * D + ot * 512 + 512],
                pD[:, sl * 512:sl * 512 + 512], AF.Exp,
                bias=bcol[:, rc:rc + 1], scale=xcol[:, rc:rc + 1]),
                deps=[mmD, t_bc, ld_xc], inc=True)
            t_evD[sl] = t_wr

    ln = LNAcc(em, 8, tb, rows, pS, pB, onesc32, onesrow, 1e-5)
    ln.war_b = [t_on, t_on]
    ln.war_mu = mmn
    ln.war_ms = mmn
    x23 = x2_sp.rearrange("(c p) s -> c p s", p=128)
    x2b3 = x2b_sp.rearrange("(c p) s -> c p s", p=128)
    x33 = x3_sp.rearrange("(c p) s -> c p s", p=128)
    x3b3 = x3b_sp.rearrange("(c p) s -> c p s", p=128)
    t_radd = [None] * 8
    t_ap3 = None
    t_nx = t_wr
    t_ft = [t_wr] * 8
    t_evW = [None, None]
    sp1 = sp2 = None
    for t in range(NT):
        ld_x = em.dma("dsync", lambda e, t=t: e.dma_start(
            out=xe[:, :].rearrange("p (c s) -> p c s", s=TILE),
            in_=x23[:, :, t * TILE:(t + 1) * TILE].rearrange("c p s -> p c s")),
            deps=[t_radd[7], t_nx])
        ld_xb = em.dma("dsync", lambda e, t=t: e.dma_start(
            out=xb[:, :].rearrange("p (c s) -> p c s", s=TILE),
            in_=x2b3[:, :, t * TILE:(t + 1) * TILE].rearrange("c p s -> p c s")),
            deps=[em.tok("tensor")])
        t_xlo = em.on("vector", lambda e: e.tensor_sub(xlo[:, :], xe[:, :], xb[:, :]),
                      deps=[ld_x, ld_xb], inc=True)
        mmx = None
        t_sx = None
        for k in range(8):
            t_sx = em.on("scalar", lambda e, k=k: e.activation(
                tb[:, :], xe[:, k * TILE:(k + 1) * TILE], AF.Square),
                deps=[ld_x, mmx], inc=True)
            mmx = em.on("tensor", lambda e, k=k: e.matmul(
                pS[0:1, 0:512], onesc32[:, 0:1], tb[:, :],
                start=(k == 0), stop=(k == 7)),
                deps=[t_sx] + ([ln.war_mu] if k == 0 else []), inc=True)
        ln.war_mu = None
        t_nx = em.on("vector", lambda e: e.tensor_scalar(
            nx[:, :], pS[0:1, 0:512], 1.0, None, ALU.mult), deps=[mmx], inc=True)
        for rc in range(8):
            sl = rc % 2
            mmD = None
            for pp, (lhs_, rhs_) in enumerate(((pj_bf, xb), (pj_bf, xlo), (pj_lo, xb))):
                for ic in range(8):
                    mmD = em.on("tensor", lambda e, rc=rc, ic=ic, sl=sl, pp=pp, lhs_=lhs_, rhs_=rhs_: e.matmul(
                        pD[:, sl * 512:sl * 512 + 512],
                        lhs_[:, ic * R + rc * 128:ic * R + (rc + 1) * 128],
                        rhs_[:, ic * TILE:(ic + 1) * TILE],
                        start=(pp == 0 and ic == 0), stop=False),
                        deps=([ld_xb, t_xlo, t_ft[rc - 2]] if pp == 0 and ic == 0 else []))
            mmD = em.on("tensor", lambda e, rc=rc, sl=sl: e.matmul(
                pD[:, sl * 512:sl * 512 + 512], urow[0:1, rc * 128:(rc + 1) * 128],
                nx[0:1, 0:512], start=False, stop=True), deps=[t_nx], inc=True)
            t_ft[rc] = em.on("scalar", lambda e, rc=rc, sl=sl: e.activation(
                feats[:, rc * TILE:(rc + 1) * TILE], pD[:, sl * 512:sl * 512 + 512],
                AF.Exp, bias=bcol[:, rc:rc + 1], scale=xcol[:, rc:rc + 1]),
                deps=[mmD], inc=True)
        for oc in range(8):
            so = oc % 2
            mmW = None
            for rc in range(8):
                mmW = em.on("tensor", lambda e, oc=oc, rc=rc, so=so: e.matmul(
                    pW[:, so * 512:so * 512 + TILE],
                    wrfs[:, rc * D + oc * 128:rc * D + (oc + 1) * 128],
                    feats[:, rc * TILE:(rc + 1) * TILE],
                    start=(rc == 0), stop=(rc == 7)),
                    deps=([t_ft[7], t_evW[so]] if rc == 0 else []),
                    inc=(rc == 7))
            t_radd[oc] = em.on("vector", lambda e, oc=oc, so=so: e.tensor_add(
                r32[:, oc * TILE:(oc + 1) * TILE], pW[:, so * 512:so * 512 + TILE],
                xe[:, oc * TILE:(oc + 1) * TILE]), deps=[mmW, t_ap3], inc=True)
            t_evW[so] = t_radd[oc]
            ln.add(r32[:, oc * TILE:(oc + 1) * TILE], t_radd[oc])
        ln.finish()
        for oc in range(8):
            dep_sp = [sp1, sp2] if oc < 2 else []
            t2, t3 = ln.apply(em, r32[:, oc * TILE:(oc + 1) * TILE],
                              x3st[:, oc * TILE:(oc + 1) * TILE],
                              x3bs[:, oc * TILE:(oc + 1) * TILE], dep_extra=dep_sp)
        t_ap3 = t3
        sp1 = em.dma("dscalar", lambda e, t=t: e.dma_start(
            out=x33[:, :, t * TILE:(t + 1) * TILE].rearrange("c p s -> p c s"),
            in_=x3st[:, :].rearrange("p (c s) -> p c s", s=TILE)), deps=[t3])
        sp2 = em.dma("dscalar", lambda e, t=t: e.dma_start(
            out=x3b3[:, :, t * TILE:(t + 1) * TILE].rearrange("c p s -> p c s"),
            in_=x3bs[:, :].rearrange("p (c s) -> p c s", s=TILE)), deps=[t3])



TILE_F = 256
NT_F = S // TILE_F

def phase_F(em, g, es):
    nc = em.nc
    rmblk, rupblk, cpT = cinp(g, "rmblk"), cinp(g, "rupblk"), cinp(g, "cpT")
    x3_sp = g.sread("x3f", (D, S), F32)
    x3b_sp = g.sread("x3b", (D, S), BF16)
    out_sp = g.out("outT", (D, S), F32)
    cp_bf = g.scratch("cpbf16", (GH, D), BF16)

    sb = lambda nm, shape, dt: es.enter_context(nc.sbuf_tensor(nm, shape, dt))
    TF = TILE_F
    rm_bf = sb("f_rm", [128, 32 * 128], BF16)
    rup_bf = sb("f_rup", [128, 128 * 128], BF16)
    stage = sb("f_stage", [128, 2 * 4096], F32)
    stbf = sb("f_stbf", [128, 2 * 4096], BF16)
    ghb = sb("f_gh", [128, 128 * TF], BF16)
    cpb = sb("f_cpb", [128, 2 * 8 * 512], BF16)
    ybuf = sb("f_y", [128, 2 * TF], BF16)
    xb = sb("f_xb", [128, 8 * TF], BF16)
    xe = sb("f_xe", [128, 8 * TF], F32)
    r32 = sb("f_r32", [128, 8 * TF], F32)
    outst = sb("f_out", [128, 8 * TF], F32)
    tb = sb("f_tb", [128, TF], F32)
    rows = sb("f_rows", [1, 4 * 512], F32)
    onesc32 = sb("f_on1", [128, 1], F32)
    onesrow = sb("f_on2", [1, 128], F32)
    pYZ = es.enter_context(nc.psum_tensor("f_pYZ", [128, 2048], F32))
    pC = es.enter_context(nc.psum_tensor("f_pC", [128, 2048], F32))

    em.on("vector", lambda e: e.memset(onesc32[:, :], 1.0))
    t_on = em.on("vector", lambda e: e.memset(onesrow[:, :], 1.0), inc=True)
    rm3 = rmblk.rearrange("h c p n -> (h c) p n")
    t_rm = _load_cast_weight(em, "rm", rm3, 32, 128, stage, rm_bf)
    rup3 = rupblk.rearrange("h c j p n -> (h c j) p n")
    t_rup = _load_cast_weight(em, "rup", rup3, 128, 128, stage, rup_bf, after=t_rm[-1])
    cp3 = cpT.rearrange("(g p) o -> g p o", p=128)
    cpb3 = cp_bf.rearrange("(g p) o -> g p o", p=128)
    st_c = [None, None]
    tc_c = [None, None]
    for grp in range(32):
        b = grp % 2
        ldc = em.dma("dsync", lambda e, grp=grp, b=b: e.dma_start(
            out=stage[:, b * 4096:(b + 1) * 4096].rearrange("p (c o) -> p c o", o=D),
            in_=cp3[4 * grp:4 * grp + 4].rearrange("c p o -> p c o")),
            deps=[tc_c[b], t_rup[-1] if grp < 2 else None])
        tc_c[b] = em.on("vector", lambda e, b=b: e.tensor_copy(
            stbf[:, b * 4096:(b + 1) * 4096], stage[:, b * 4096:(b + 1) * 4096]),
            deps=[ldc, st_c[b]], inc=True)
        st_c[b] = em.dma("dscalar", lambda e, grp=grp, b=b: e.dma_start(
            out=cpb3[4 * grp:4 * grp + 4].rearrange("c p o -> p c o"),
            in_=stbf[:, b * 4096:(b + 1) * 4096].rearrange("p (c o) -> p c o", o=D)),
            deps=[tc_c[b]])

    ln = LNAcc(em, 8, tb, rows, pYZ, pYZ[:, 1024:2048], onesc32, onesrow, 1e-5, w=TF)
    ln.war_b = [t_on, t_on]
    x33 = x3_sp.rearrange("(c p) s -> c p s", p=128)
    x3b3 = x3b_sp.rearrange("(c p) s -> c p s", p=128)
    o3 = out_sp.rearrange("(c p) s -> c p s", p=128)
    TFW = TF
    t_gel = [None, None]
    t_evY = [None, None]
    t_mmZ = [None, None]
    t_cpmm = [None, None]
    t_radd = [None] * 8
    t_ap = None
    t_ghlast = None
    sp_out = None
    for t in range(NT_F):
        ld_xb = em.dma("dsync", lambda e, t=t: e.dma_start(
            out=xb[:, :].rearrange("p (c s) -> p c s", s=TF),
            in_=x3b3[:, :, t * TF:(t + 1) * TF].rearrange("c p s -> p c s")),
            deps=[em.tok("tensor")])
        ld_x = em.dma("dsync", lambda e, t=t: e.dma_start(
            out=xe[:, :].rearrange("p (c s) -> p c s", s=TF),
            in_=x33[:, :, t * TF:(t + 1) * TF].rearrange("c p s -> p c s")),
            deps=[t_radd[7]])
        kidx = 0
        for h in range(4):
            for ic in range(8):
                p_ = kidx % 2
                mmY = em.on("tensor", lambda e, h=h, ic=ic, p_=p_: e.matmul(
                    pYZ[:, p_ * 512:p_ * 512 + TFW],
                    rm_bf[:, (h * 8 + ic) * 128:(h * 8 + ic + 1) * 128],
                    xb[:, ic * TF:(ic + 1) * TF], start=True, stop=True),
                    deps=[ld_xb, t_rm[-1], t_evY[p_]] +
                         ([ln.war_mu, ln.war_ms] if kidx == 0 else []), inc=True)
                t_evY[p_] = em.on("vector", lambda e, p_=p_: e.tensor_copy(
                    ybuf[:, p_ * TF:(p_ + 1) * TF], pYZ[:, p_ * 512:p_ * 512 + TFW]),
                    deps=[mmY, t_mmZ[p_]], inc=True)
                for js in range(4):
                    zp = js % 2
                    t_mmZ[p_] = em.on("tensor", lambda e, h=h, ic=ic, js=js, p_=p_, zp=zp: e.matmul(
                        pYZ[:, 1024 + zp * 512:1024 + zp * 512 + TFW],
                        rup_bf[:, ((h * 8 + ic) * 4 + js) * 128:((h * 8 + ic) * 4 + js + 1) * 128],
                        ybuf[:, p_ * TF:(p_ + 1) * TF], start=True, stop=True),
                        deps=[t_evY[p_], t_rup[-1], t_gel[zp]] +
                             ([t_ap] if kidx == 0 and js < 2 else []), inc=True)
                    kc = (h * 8 + ic) * 4 + js
                    t_gel[zp] = em.on("scalar", lambda e, kc=kc, zp=zp: e.activation(
                        ghb[:, kc * TF:(kc + 1) * TF],
                        pYZ[:, 1024 + zp * 512:1024 + zp * 512 + TFW], AF.Gelu),
                        deps=[t_mmZ[p_], t_ghlast if kc < 2 else None], inc=True)
                kidx += 1
        t_ghfirst = t_gel[0]
        t_evC = [None] * 4
        for half in range(2):
            lds = [None, None]
            mmlast = None
            for g16 in range(16):
                b = g16 % 2
                lds[b] = em.dma("dsync", lambda e, g16=g16, b=b, half=half: e.dma_start(
                    out=cpb[:, b * 4096:(b + 1) * 4096].rearrange("p (c o) -> p c o", o=512),
                    in_=cpb3[8 * g16:8 * g16 + 8, :, half * 512:(half + 1) * 512].rearrange("c p o -> p c o")),
                    deps=[t_cpmm[b], st_c[0], st_c[1]])
                for kcg in range(8):
                    kc = g16 * 8 + kcg
                    for oc4 in range(4):
                        mmlast = em.on("tensor", lambda e, b=b, kcg=kcg, oc4=oc4, kc=kc: e.matmul(
                            pC[:, oc4 * 512:oc4 * 512 + TFW],
                            cpb[:, b * 4096 + kcg * 512 + oc4 * 128:b * 4096 + kcg * 512 + (oc4 + 1) * 128],
                            ghb[:, kc * TF:(kc + 1) * TF],
                            start=(kc == 0), stop=(kc == 127)),
                            deps=([lds[b]] if kcg == 0 and oc4 == 0 else []) +
                                 ([t_gel[0], t_gel[1]] if kc == 0 and oc4 == 0 else []) +
                                 ([t_evC[oc4]] if kc == 0 else []), inc=(kcg == 7 and oc4 == 3))
                t_cpmm[b] = mmlast
            for oc4 in range(4):
                oc = half * 4 + oc4
                t_radd[oc] = em.on("vector", lambda e, oc=oc, oc4=oc4: e.tensor_add(
                    r32[:, oc * TF:(oc + 1) * TF], pC[:, oc4 * 512:oc4 * 512 + TFW],
                    xe[:, oc * TF:(oc + 1) * TF]), deps=[mmlast, ld_x, t_ap], inc=True)
                t_evC[oc4] = t_radd[oc]
                ln.add(r32[:, oc * TF:(oc + 1) * TF], t_radd[oc])
        ln.finish()
        for oc in range(8):
            t_ap, _ = ln.apply(em, r32[:, oc * TF:(oc + 1) * TF],
                               outst[:, oc * TF:(oc + 1) * TF], None,
                               dep_extra=[sp_out] if oc < 2 else [])
        t_ghlast = t_ap
        sp_out = em.dma("dscalar", lambda e, t=t: e.dma_start(
            out=o3[:, :, t * TF:(t + 1) * TF].rearrange("c p s -> p c s"),
            in_=outst[:, :].rearrange("p (c s) -> p c s", s=TF)), deps=[t_ap])
PHASE_FNS = {}
PHASE_FNS["A1"] = phase_A1
PHASE_FNS["A2"] = phase_A2
PHASE_FNS["B"] = phase_B
PHASE_FNS["C"] = phase_C
PHASE_FNS["D1"] = phase_D1
PHASE_FNS["D2"] = phase_D2
PHASE_FNS["E"] = phase_E
PHASE_FNS["F"] = phase_F
PHASE_ORDER = ["A1", "A2", "B", "C", "D1", "D2", "E", "F"]



def _build(phases, fused):
    nc = bass.Bass("TRN2", target_bir_lowering=False)
    es = ExitStack()
    g = G(nc, fused)
    em = Emit(nc, es)
    for i, p in enumerate(phases):
        pes = ExitStack()
        Emit.SERIAL = (p in SERIAL_PHASES)
        PHASE_FNS[p](em, g, pes)
        em.barrier()
        pes.close()
    em.finalize()
    es.close()
    return nc, g


def kernel(**inputs):
    phases = PHASES or PHASE_ORDER
    consts = host_prepare(inputs)
    x = np.asarray(inputs["L0_attn_x"])
    percore = [{"xT": np.ascontiguousarray(x[c].T)} for c in range(NCORE)]

    if FUSED:
        nc, g = _build(phases, True)
        in_maps = []
        for c in range(NCORE):
            m = {}
            for name in g.kinds:
                if g.kinds[name] != "ExternalInput":
                    continue
                m[name] = percore[c].get(name, consts.get(name))
                assert m[name] is not None, name
            in_maps.append(m)
        res = run_bass_kernel_spmd(nc, in_maps, core_ids=list(range(NCORE)))
        outs = [r["outT"] for r in res.results]
        return np.stack([o.T for o in outs]).astype(np.float32)

    cores = CORES if CORES is not None else list(range(NCORE))
    env = [dict(percore[c]) for c in cores]
    final = None
    for p in phases:
        nc, g = _build([p], False)
        in_maps = []
        for i, c in enumerate(cores):
            m = {}
            for name, kind in g.kinds.items():
                if kind != "ExternalInput":
                    continue
                v = env[i].get(name)
                if v is None:
                    v = consts.get(name)
                assert v is not None, f"missing input {name} for phase {p}"
                m[name] = v
            in_maps.append(m)
        res = run_bass_kernel_spmd(nc, in_maps, core_ids=cores)
        for i in range(len(cores)):
            for name, kind in g.kinds.items():
                if kind == "ExternalOutput":
                    env[i][name] = np.asarray(res.results[i][name])
        final = [env[i].get("outT") for i in range(len(cores))]
    if final is not None and final[0] is not None:
        return np.stack([o.T for o in final]).astype(np.float32)
    return env


# version 05
# speedup vs baseline: 1.0050x; 1.0050x over previous
import numpy as np
import ml_dtypes
import concourse.bass as bass
import concourse.mybir as mybir
from concourse.bass_utils import run_bass_kernel_spmd
from contextlib import ExitStack

F32 = mybir.dt.float32
BF16 = mybir.dt.bfloat16
I32 = mybir.dt.int32
AF = mybir.ActivationFunctionType
ALU = mybir.AluOpType

NCORE = 8
S = 4096
D = 1024
INNER = 2048
NH = 4
DH = 512
HID = 3072
R = 1024
GH = 16384
TILE = 512
NT = S // TILE
LAM = 0.5 * float(np.log(DH))

FUSED = True
DEBUG_B = False
SERIAL_PHASES = set()
B_SER = ""
STREAMS_OUT = True
DEBUG_B_TILE = (3, 7)
PHASES = None
CORES = None

ENGINES = ("tensor", "vector", "scalar", "gpsimd", "sync")
DQUEUE = {"dsync": "sync", "dscalar": "scalar", "dgp": "gpsimd"}


class Tok:
    __slots__ = ("sem", "val")
    def __init__(self, sem, val):
        self.sem = sem
        self.val = val


class Emit:
    def __init__(self, nc, es):
        self.nc = nc
        self.ops = {e: [] for e in ENGINES}
        self.sem = {}
        self.cnt = {}
        for e in ENGINES:
            self.sem[e] = es.enter_context(nc.semaphore(f"sp_{e}"))
            self.cnt[e] = 0
        for q in DQUEUE:
            self.sem[q] = es.enter_context(nc.semaphore(f"sq_{q}"))
            self.cnt[q] = 0
        self.bar = es.enter_context(nc.semaphore("sbar"))
        self.bar_cnt = 0
        self.fences = {e: [] for e in ("vector", "scalar", "gpsimd")}
        self.fencebuf = es.enter_context(nc.sbuf_tensor("fencebuf", [1, 8], F32))
        self.ops["vector"].append(("o", lambda e: e.memset(self.fencebuf[:, :], 0.0), None))
        self.vc = {e: {s: 0 for s in self.sem} for e in ENGINES}

    FENCED = ("vector", "scalar", "gpsimd")

    def _fence_target(self, peng, val):
        for cov, fval in self.fences[peng]:
            if cov >= val:
                return fval
        cov = self.cnt[peng]
        self.cnt[peng] += 1
        fval = self.cnt[peng]
        fb = self.fencebuf
        if peng == "scalar":
            fn = lambda e: e.mul(fb[0:1, 4:8], fb[0:1, 0:4], 1.0)
        else:
            fn = lambda e: e.memset(fb[0:1, 0:4], 0.0)
        self.ops[peng].append(("o", fn, peng))
        self.fences[peng].append((cov, fval))
        return fval

    def _waits(self, eng, deps, dma=False):
        for t in deps:
            if t is None:
                continue
            if t.sem == eng and not dma:
                continue
            v = t.val
            s = t.sem
            if s == "dgp":
                v = self.cnt["dgp"]
            elif s in Emit.FENCED and (s != eng or dma):
                v = self._fence_target(s, t.val)
            if self.vc[eng][s] < v:
                self.vc[eng][s] = v
                self.ops[eng].append(("w", s, v))

    SERIAL = True

    def on(self, eng, fn, deps=(), inc=False, hard=()):
        if Emit.SERIAL or getattr(self, 'serial_now', False):
            inc = True
            deps = list(deps) + [Tok(k, self.cnt[k]) for k in self.cnt]
            self._waits(eng, deps, dma=True)
            self.cnt[eng] += 1
            self.ops[eng].append(("o", fn, eng))
            return Tok(eng, self.cnt[eng])
        self._waits(eng, deps)
        self._waits(eng, hard, dma=True)
        tok = None
        if inc:
            self.cnt[eng] += 1
            tok = Tok(eng, self.cnt[eng])
        self.ops[eng].append(("o", fn, eng if inc else None))
        return tok

    def dma(self, queue, fn, deps=()):
        eng = DQUEUE[queue]
        if Emit.SERIAL or getattr(self, 'serial_now', False):
            deps = list(deps) + [Tok(k, self.cnt[k]) for k in self.cnt]
        self._waits(eng, deps, dma=True)
        self.cnt[queue] += 16
        self.ops[eng].append(("o", fn, queue))
        return Tok(queue, self.cnt[queue])

    def tok(self, key):
        return Tok(key, self.cnt[key])

    def barrier(self):
        for q, eng in DQUEUE.items():
            if self.vc[eng][q] < self.cnt[q]:
                self.vc[eng][q] = self.cnt[q]
                self.ops[eng].append(("w", q, self.cnt[q]))
        self.bar_cnt += 1
        target = 5 * self.bar_cnt
        for e in ENGINES:
            self.ops[e].append(("b",))
        for e in ENGINES:
            self.ops[e].append(("w", "bar", target))
            for s in self.sem:
                self.vc[e][s] = max(self.vc[e][s], self.cnt[s])
            self.vc[e]["bar"] = target

    def finalize(self):
        with self.nc.Block() as block:
            for e in ENGINES:
                oplist = self.ops[e]
                if not oplist:
                    continue
                def runner(engobj, oplist=oplist):
                    for item in oplist:
                        if item[0] == "w":
                            sem = self.bar if item[1] == "bar" else self.sem[item[1]]
                            engobj.wait_ge(sem, item[2])
                        elif item[0] == "b":
                            engobj.drain().then_inc(self.bar, 1)
                        else:
                            ins = item[1](engobj)
                            if item[2] is not None:
                                if item[2] in DQUEUE:
                                    ins.then_inc(self.sem[item[2]], 16)
                                else:
                                    ins.then_inc(self.sem[item[2]], 1)
                getattr(block, e)(runner)


class G:
    def __init__(self, nc, fused):
        self.nc = nc
        self.fused = fused
        self.h = {}
        self.kinds = {}

    def _get(self, name, shape, dtype, kind):
        if name in self.h:
            return self.h[name]
        t = self.nc.dram_tensor(name, list(shape), dtype, kind=kind).ap()
        self.h[name] = t
        self.kinds[name] = kind
        return t

    def inp(self, name, shape, dtype):
        return self._get(name, shape, dtype, "ExternalInput")

    def out(self, name, shape, dtype):
        return self._get(name, shape, dtype, "ExternalOutput")

    def sread(self, name, shape, dtype):
        return self._get(name, shape, dtype,
                         "Internal" if self.fused else "ExternalInput")

    def swrite(self, name, shape, dtype):
        if self.fused:
            kind = "ExternalOutput" if STREAMS_OUT else "Internal"
        else:
            kind = "ExternalOutput"
        return self._get(name, shape, dtype, kind)

    def scratch(self, name, shape, dtype):
        return self._get(name, shape, dtype, "Internal")



def _blockdiag(W, transpose):
    out = np.zeros((16, 128, 128), dtype=np.float32)
    for c in range(16):
        for j in range(32):
            h = 32 * c + j
            blk = W[h] if transpose else W[h].T
            out[c, 4 * j:4 * j + 4, 4 * j:4 * j + 4] = blk
    return out


def _cols(v, n):
    return np.ascontiguousarray(v.reshape(n, 128).T)


def _rm_blk(rm):
    out = np.zeros((4, 8, 128, 128), dtype=np.float32)
    for h in range(4):
        for c in range(8):
            for j in range(8):
                out[h, c, 16 * j:16 * j + 16, 16 * j:16 * j + 16] = rm[h, 8 * c + j].T
    return out


def _rup_blk(rup):
    out = np.zeros((4, 8, 4, 128, 128), dtype=np.float32)
    for h in range(4):
        for c in range(8):
            for js in range(4):
                for j in range(8):
                    blk = rup[h, 8 * c + j, 16 * js:16 * js + 16, :].T
                    out[h, c, js, 16 * j:16 * j + 16, 16 * j:16 * j + 16] = blk
    return out


def _cproj_perm():
    perm = np.empty(GH, dtype=np.int64)
    i = 0
    for h in range(4):
        for c in range(8):
            for js in range(4):
                for j in range(8):
                    for kk in range(16):
                        perm[i] = h * 4096 + (8 * c + j) * 64 + 16 * js + kk
                        i += 1
    return perm


def _masks():
    m = np.zeros((5, 128, 512), dtype=np.float32)
    for mm in range(4):
        t = np.arange(128)[:, None] + 128 * mm
        s = np.arange(512)[None, :]
        m[mm] = (t <= s).astype(np.float32)
    m[4] = 1.0
    return m


def host_prepare(inputs):
    c = {}
    c["wup"] = np.ascontiguousarray(inputs["L0_attn_W_up"], dtype=np.float32)
    c["bq"] = _blockdiag(np.asarray(inputs["L0_attn_Wq"]), False)
    c["bk"] = _blockdiag(np.asarray(inputs["L0_attn_Wk"]), False)
    c["bv"] = _blockdiag(np.asarray(inputs["L0_attn_Wv"]), False)
    c["bqT"] = _blockdiag(np.asarray(inputs["L0_attn_Wq"]), True)
    c["bkT"] = _blockdiag(np.asarray(inputs["L0_attn_Wk"]), True)
    c["bvT"] = _blockdiag(np.asarray(inputs["L0_attn_Wv"]), True)
    c["wig"] = np.ascontiguousarray(inputs["L0_attn_W_ig"], dtype=np.float32)
    c["wfg"] = np.ascontiguousarray(inputs["L0_attn_W_fg"], dtype=np.float32)
    c["biasig"] = np.asarray(inputs["L0_attn_b_ig"]).reshape(4, 1).astype(np.float32)
    c["biasfg"] = np.asarray(inputs["L0_attn_b_fg"]).reshape(4, 1).astype(np.float32)
    cw = np.zeros((128, 64), dtype=np.float32)
    for cc in range(16):
        for i in range(4):
            cw[:, 4 * cc + i] = inputs["L0_attn_conv_w"][i, 128 * cc:128 * cc + 128]
    c["convw"] = cw
    c["convb"] = _cols(np.asarray(inputs["L0_attn_conv_b"]), 16)
    c["nw"] = _cols(np.asarray(inputs["L0_attn_norm_w"]), 16)
    c["sk"] = _cols(np.asarray(inputs["L0_attn_skip"]), 16)
    c["wdown"] = np.ascontiguousarray(inputs["L0_attn_W_down"], dtype=np.float32)
    c["wgT"] = np.ascontiguousarray(np.asarray(inputs["L0_ffn_w_gate"]).T)
    c["wuT"] = np.ascontiguousarray(np.asarray(inputs["L0_ffn_w_up"]).T)
    c["wdT"] = np.ascontiguousarray(np.asarray(inputs["L0_ffn_w_down"]).T)
    c["iwT"] = np.ascontiguousarray(np.asarray(inputs["L1_attn_input_weights"]).T)
    c["projT"] = np.ascontiguousarray(np.asarray(inputs["L1_attn_proj_matrix"]).T)
    c["xisrow"] = np.ascontiguousarray(np.asarray(inputs["L1_attn_xis"]).reshape(1, R))
    c["xiscols"] = _cols(np.asarray(inputs["L1_attn_xis"]), 8)
    c["rmblk"] = _rm_blk(np.asarray(inputs["L1_ffn_rail_maker"]))
    c["rupblk"] = _rup_blk(np.asarray(inputs["L1_ffn_rail_up_proj"]))
    c["cpT"] = np.ascontiguousarray(np.asarray(inputs["L1_ffn_w_cproj"]).T[_cproj_perm()])
    c["ident"] = np.eye(128, dtype=np.float32)
    c["masks"] = _masks()
    return c


CONST_SHAPES = {
    "wup": ((D, 4096), F32), "bq": ((16, 128, 128), F32), "bk": ((16, 128, 128), F32),
    "bv": ((16, 128, 128), F32), "bqT": ((16, 128, 128), F32), "bkT": ((16, 128, 128), F32),
    "bvT": ((16, 128, 128), F32), "wig": ((6144, 4), F32), "wfg": ((6144, 4), F32),
    "biasig": ((4, 1), F32), "biasfg": ((4, 1), F32), "convw": ((128, 64), F32), "convb": ((128, 16), F32),
    "nw": ((128, 16), F32), "sk": ((128, 16), F32), "wdown": ((INNER, D), F32),
    "wgT": ((D, HID), I32), "wuT": ((D, HID), I32), "wdT": ((HID, D), I32),
    "iwT": ((D, D), F32), "projT": ((D, D), F32), "xisrow": ((1, R), F32),
    "xiscols": ((128, 8), F32), "rmblk": ((4, 8, 128, 128), F32),
    "rupblk": ((4, 8, 4, 128, 128), F32), "cpT": ((GH, D), F32),
    "ident": ((128, 128), F32), "masks": ((5, 128, 512), F32),
}


def cinp(g, name):
    shape, dtype = CONST_SHAPES[name]
    return g.inp(name, shape, dtype)



def phase_A1(em, g, es):
    nc = em.nc
    xT = g.inp("xT", (D, S), F32)
    wup = cinp(g, "wup")
    bq, bk, bv = cinp(g, "bq"), cinp(g, "bk"), cinp(g, "bv")
    bqT, bkT, bvT = cinp(g, "bqT"), cinp(g, "bkT"), cinp(g, "bvT")
    wig, wfg = cinp(g, "wig"), cinp(g, "wfg")
    convw, convb = cinp(g, "convw"), cinp(g, "convb")

    xact_sp = g.swrite("xact", (INNER, S), F32)
    q_sp = g.swrite("qfm", (INNER, S), BF16)
    k_sp = g.swrite("kfm", (INNER, S), BF16)
    v_sp = g.swrite("vtm", (S, INNER), BF16)
    gates_sp = g.swrite("gates8", (8, S), F32)

    sb = lambda nm, shape, dt: es.enter_context(nc.sbuf_tensor(nm, shape, dt))
    ps = lambda nm, shape: es.enter_context(nc.psum_tensor(nm, shape, F32))

    wup_bf = sb("a1_wupbf", [128, 8 * INNER], BF16)
    stage32 = sb("a1_stage", [128, 2 * INNER], F32)
    blkq = sb("a1_blkq", [128, 2048], BF16)
    blkk = sb("a1_blkk", [128, 2048], BF16)
    blkv = sb("a1_blkv", [128, 2048], BF16)
    wigfg = sb("a1_wigfg", [128, 48 * 8], F32)
    wfold_a = sb("a1_wfa", [128, 16 * 8], F32)
    wfold_v = sb("a1_wfv", [128, 16 * 8], F32)
    cw = sb("a1_cw", [128, 64], F32)
    cb = sb("a1_cb", [128, 16], F32)
    convbuf = sb("a1_conv", [128, 16 * 515], F32)
    xct = sb("a1_xct", [128, TILE], F32)
    xe = sb("a1_xe", [128, 8 * TILE], F32)
    xb = sb("a1_xb", [128, 8 * TILE], BF16)
    xa32 = sb("a1_xa32", [128, 2 * TILE], F32)
    xab = sb("a1_xab", [128, 2 * TILE], BF16)
    xmb = sb("a1_xmb", [128, 2 * TILE], BF16)
    qst = sb("a1_qst", [128, 16 * TILE], BF16)
    kst = sb("a1_kst", [128, 16 * TILE], BF16)
    vst = sb("a1_vst", [128, 4 * INNER], BF16)
    gac = sb("a1_gac", [8, S], F32)

    pW = ps("a1_pW", [128, 1024])
    pQK = ps("a1_pQK", [128, 1024])
    pV = ps("a1_pV", [128, 1024])
    pG = ps("a1_pG", [8, TILE])

    wup3 = wup.rearrange("(c p) n -> c p n", p=128)
    t_wup = [None] * 8
    for ic in range(8):
        ld = em.dma("dsync", lambda e, ic=ic: e.dma_start(
            out=stage32[:, (ic % 2) * INNER:(ic % 2 + 1) * INNER],
            in_=wup3[ic, :, 0:INNER]),
            deps=[t_wup[ic - 2]] if ic >= 2 else [])
        t_wup[ic] = em.on("vector", lambda e, ic=ic: e.tensor_copy(
            wup_bf[:, ic * INNER:(ic + 1) * INNER],
            stage32[:, (ic % 2) * INNER:(ic % 2 + 1) * INNER]), deps=[ld], inc=True)
    t_cast = []
    for i, (src, dst) in enumerate(((bq, blkq), (bk, blkk), (bv, blkv))):
        ld = em.dma("dsync", lambda e, src=src, i=i: e.dma_start(
            out=stage32[:, (i % 2) * 2048:(i % 2 + 1) * 2048],
            in_=src.rearrange("c p n -> p c n")),
            deps=[t_wup[6 + (i % 2)]] + ([t_cast[i - 2]] if i >= 2 else []))
        t_cast.append(em.on("vector", lambda e, dst=dst, i=i: e.tensor_copy(
            dst[:, :], stage32[:, (i % 2) * 2048:(i % 2 + 1) * 2048]), deps=[ld], inc=True))
    wv8 = wigfg[:, :].rearrange("p (c gg) -> p c gg", gg=8)
    t_wg1 = em.dma("dsync", lambda e: e.dma_start(
        out=wv8[:, :, 0:4], in_=wig.rearrange("(c p) g -> p c g", p=128)))
    t_wg2 = em.dma("dsync", lambda e: e.dma_start(
        out=wv8[:, :, 4:8], in_=wfg.rearrange("(c p) g -> p c g", p=128)))
    t_cw = em.dma("dsync", lambda e: e.dma_start(out=cw[:, :], in_=convw))
    t_cb = em.dma("dsync", lambda e: e.dma_start(out=cb[:, :], in_=convb))
    ldq = em.dma("dsync", lambda e: e.dma_start(
        out=stage32[:, 0:2048], in_=bqT.rearrange("c p n -> p c n")),
        deps=[t_cast[2]])
    mm = None
    for c in range(16):
        mm = em.on("tensor", lambda e, c=c: e.matmul(
            pV[:, 8 * c:8 * c + 8], stage32[:, c * 128:(c + 1) * 128],
            wigfg[:, c * 8:(c + 1) * 8], start=(c == 0), stop=False),
            deps=[ldq, t_wg1, t_wg2] if c == 0 else [], inc=(c == 15))
    ldk = em.dma("dsync", lambda e: e.dma_start(
        out=stage32[:, 0:2048], in_=bkT.rearrange("c p n -> p c n")), deps=[mm])
    for c in range(16):
        mm = em.on("tensor", lambda e, c=c: e.matmul(
            pV[:, 8 * c:8 * c + 8], stage32[:, c * 128:(c + 1) * 128],
            wigfg[:, (16 + c) * 8:(17 + c) * 8], start=False, stop=True),
            deps=[ldk] if c == 0 else [], inc=(c == 15))
    t_folda = em.on("vector", lambda e: e.tensor_copy(
        wfold_a[:, :], pV[:, 0:128]), deps=[mm], inc=True)
    ldv = em.dma("dsync", lambda e: e.dma_start(
        out=stage32[:, 0:2048], in_=bvT.rearrange("c p n -> p c n")), deps=[mm])
    for c in range(16):
        mm = em.on("tensor", lambda e, c=c: e.matmul(
            pV[:, 512 + 8 * c:512 + 8 * c + 8], stage32[:, c * 128:(c + 1) * 128],
            wigfg[:, (32 + c) * 8:(33 + c) * 8], start=(c == 0), stop=True),
            deps=[ldv] if c == 0 else [], inc=(c == 15))
    t_foldv = em.on("vector", lambda e: e.tensor_copy(
        wfold_v[:, :], pV[:, 512:640]), deps=[mm], inc=True)
    t_zero = em.on("vector", lambda e: e.memset(convbuf[:, :], 0.0), inc=True)

    cb3 = convbuf[:, :].rearrange("p (c n) -> p c n", n=515)
    xT3 = xT.rearrange("(c p) s -> c p s", p=128)
    q3 = q_sp.rearrange("(c p) s -> c p s", p=128)
    k3 = k_sp.rearrange("(c p) s -> c p s", p=128)
    xa3 = xact_sp.rearrange("(c p) s -> c p s", p=128)
    v3 = v_sp.rearrange("(sc p) d -> sc p d", p=128)

    tx_b = None
    t_evW = [None, None]
    t_silu = [None] * 16
    t_slb = [None] * 16
    t_store_xa = [None] * 16
    t_mmG = [None] * 16
    t_mmq = [None] * 16
    t_mmk = [None] * 16
    t_mmv = [None] * 16
    t_evv_slot = [t_folda, t_foldv]
    war_q = war_k = None
    t_spv = [None] * 4
    t_gacev = None
    for t in range(NT):
        ld_x = em.dma("dsync", lambda e, t=t: e.dma_start(
            out=xe[:, :].rearrange("p (c s) -> p c s", s=TILE),
            in_=xT3[:, :, t * TILE:(t + 1) * TILE].rearrange("c p s -> p c s")),
            deps=[tx_b])
        tx_b = em.on("vector", lambda e: e.tensor_copy(xb[:, :], xe[:, :]),
                     deps=[ld_x, t_wup[-1], em.tok("tensor")], inc=True)
        th = em.on("vector", lambda e: e.tensor_copy(
            cb3[:, :, 0:3], cb3[:, :, 512:515]), deps=[t_zero], inc=True)
        vcount = 0
        for oc in range(16):
            so = oc % 2
            mmW = None
            for ic in range(8):
                mmW = em.on("tensor", lambda e, oc=oc, ic=ic, so=so: e.matmul(
                    pW[:, so * 512:so * 512 + TILE],
                    wup_bf[:, ic * INNER + oc * 128:ic * INNER + (oc + 1) * 128],
                    xb[:, ic * TILE:(ic + 1) * TILE],
                    start=(ic == 0), stop=(ic == 7)),
                    deps=([tx_b, t_evW[so]] if ic == 0 else []), inc=(ic == 7))
            ev = em.on("vector", lambda e, oc=oc, so=so: e.tensor_copy(
                cb3[:, oc, 3:515], pW[:, so * 512:so * 512 + TILE]),
                deps=[mmW, th, t_mmG[oc]], inc=True)
            t_evW[so] = ev
            em.on("vector", lambda e, oc=oc: e.tensor_scalar(
                xct[:, :], cb3[:, oc, 0:512], cw[:, 4 * oc:4 * oc + 1], None, ALU.mult),
                deps=[t_cw, t_silu[oc - 1], t_slb[oc - 1]])
            for i in range(1, 3):
                em.on("vector", lambda e, oc=oc, i=i: e.scalar_tensor_tensor(
                    xct[:, :], cb3[:, oc, i:i + 512], cw[:, 4 * oc + i:4 * oc + i + 1],
                    xct[:, :], ALU.mult, ALU.add))
            last_conv = em.on("vector", lambda e, oc=oc: e.scalar_tensor_tensor(
                xct[:, :], cb3[:, oc, 3:515], cw[:, 4 * oc + 3:4 * oc + 4],
                xct[:, :], ALU.mult, ALU.add), inc=True)
            t_silu[oc] = em.on("scalar", lambda e, oc=oc, so=so: e.activation(
                xa32[:, so * TILE:(so + 1) * TILE], xct[:, :], AF.Silu,
                bias=cb[:, oc:oc + 1], scale=1.0),
                deps=[last_conv, t_cb, t_store_xa[oc - 2], t_mmG[oc - 2]], inc=True)
            t_slb[oc] = em.on("scalar", lambda e, oc=oc, so=so: e.activation(
                xab[:, so * TILE:(so + 1) * TILE], xct[:, :], AF.Silu,
                bias=cb[:, oc:oc + 1], scale=1.0),
                deps=[last_conv, t_mmq[oc - 2], t_mmk[oc - 2]], inc=True)
            t_store_xa[oc] = em.dma("dscalar", lambda e, oc=oc, t=t, so=so: e.dma_start(
                out=xa3[oc, :, t * TILE:(t + 1) * TILE],
                in_=xa32[:, so * TILE:(so + 1) * TILE]), deps=[t_silu[oc]])
            txm = em.on("vector", lambda e, oc=oc, so=so: e.tensor_copy(
                xmb[:, so * TILE:(so + 1) * TILE], cb3[:, oc, 3:515]),
                deps=[ev, t_mmv[oc - 2]], inc=True)
            t_mmq[oc] = em.on("tensor", lambda e, oc=oc, so=so: e.matmul(
                pQK[:, 0:TILE], blkq[:, oc * 128:(oc + 1) * 128],
                xab[:, so * TILE:(so + 1) * TILE], start=True, stop=True),
                deps=[t_slb[oc], t_cast[0], war_q], inc=True)
            war_q = em.on("vector", lambda e, oc=oc: e.tensor_copy(
                qst[:, oc * TILE:(oc + 1) * TILE], pQK[:, 0:TILE]),
                deps=[t_mmq[oc]], inc=True)
            t_mmk[oc] = em.on("tensor", lambda e, oc=oc, so=so: e.matmul(
                pQK[:, 512:512 + TILE], blkk[:, oc * 128:(oc + 1) * 128],
                xab[:, so * TILE:(so + 1) * TILE], start=True, stop=True),
                deps=[t_cast[1], war_k], inc=True)
            war_k = em.on("vector", lambda e, oc=oc: e.tensor_copy(
                kst[:, oc * TILE:(oc + 1) * TILE], pQK[:, 512:512 + TILE]),
                deps=[t_mmk[oc]], inc=True)
            em.on("tensor", lambda e, oc=oc, so=so: e.matmul(
                pG[:, :], wfold_a[:, oc * 8:(oc + 1) * 8],
                xa32[:, so * TILE:(so + 1) * TILE],
                start=(oc == 0), stop=False),
                deps=[t_silu[oc], t_folda, t_foldv] + ([t_gacev] if oc == 0 else []))
            t_mmG[oc] = em.on("tensor", lambda e, oc=oc: e.matmul(
                pG[:, :], wfold_v[:, oc * 8:(oc + 1) * 8], cb3[:, oc, 3:515],
                start=False, stop=(oc == 15)), deps=[], inc=True)
            for sc in range(4):
                sv = vcount % 2
                mmv = em.on("tensor", lambda e, oc=oc, sc=sc, sv=sv, so=so: e.matmul(
                    pV[:, sv * 512:sv * 512 + 128],
                    xmb[:, so * TILE + sc * 128:so * TILE + (sc + 1) * 128],
                    blkv[:, oc * 128:(oc + 1) * 128], start=True, stop=True),
                    deps=[txm, t_cast[2], t_evv_slot[sv],
                          t_spv[sc] if oc == 0 else None], inc=True)
                t_mmv[oc] = mmv
                t_evv_slot[sv] = em.on("vector", lambda e, oc=oc, sc=sc, sv=sv: e.tensor_copy(
                    vst[:, sc * INNER + oc * 128:sc * INNER + (oc + 1) * 128],
                    pV[:, sv * 512:sv * 512 + 128]), deps=[mmv], inc=True)
                vcount += 1
        t_gacev = em.on("vector", lambda e, t=t: e.tensor_copy(
            gac[:, t * TILE:(t + 1) * TILE], pG[:, :]), deps=[t_mmG[15]], inc=True)
        lastv = em.tok("vector")
        spq = em.dma("dscalar", lambda e, t=t: e.dma_start(
            out=q3[:, :, t * TILE:(t + 1) * TILE].rearrange("c p s -> p c s"),
            in_=qst[:, :].rearrange("p (c s) -> p c s", s=TILE)), deps=[war_q])
        spk = em.dma("dscalar", lambda e, t=t: e.dma_start(
            out=k3[:, :, t * TILE:(t + 1) * TILE].rearrange("c p s -> p c s"),
            in_=kst[:, :].rearrange("p (c s) -> p c s", s=TILE)), deps=[war_k])
        for sc in range(4):
            t_spv[sc] = em.dma("dscalar", lambda e, t=t, sc=sc: e.dma_start(
                out=v3[4 * t + sc, :, :], in_=vst[:, sc * INNER:(sc + 1) * INNER]),
                deps=[lastv])
        war_q, war_k = spq, spk
    em.dma("dgp", lambda e: e.dma_start(out=gates_sp, in_=gac[:, :]),
           deps=[t_gacev])



def phase_A2(em, g, es):
    nc = em.nc
    xT = g.inp("xT", (D, S), F32)
    wup = cinp(g, "wup")
    sz_sp = g.swrite("sz", (INNER, S), F32)

    sb = lambda nm, shape, dt: es.enter_context(nc.sbuf_tensor(nm, shape, dt))
    wup_bf = sb("a2_wupbf", [128, 8 * INNER], BF16)
    stage32 = sb("a2_stage", [128, 2 * INNER], F32)
    xe = sb("a2_xe", [128, 8 * TILE], F32)
    xb = sb("a2_xb", [128, 8 * TILE], BF16)
    szs = sb("a2_szs", [128, 2 * TILE], F32)
    pW = es.enter_context(nc.psum_tensor("a2_pW", [128, 1024], F32))

    wup3 = wup.rearrange("(c p) n -> c p n", p=128)
    t_wup = [None] * 8
    for ic in range(8):
        ld = em.dma("dsync", lambda e, ic=ic: e.dma_start(
            out=stage32[:, (ic % 2) * INNER:(ic % 2 + 1) * INNER],
            in_=wup3[ic, :, INNER:2 * INNER]),
            deps=[t_wup[ic - 2]] if ic >= 2 else [])
        t_wup[ic] = em.on("vector", lambda e, ic=ic: e.tensor_copy(
            wup_bf[:, ic * INNER:(ic + 1) * INNER],
            stage32[:, (ic % 2) * INNER:(ic % 2 + 1) * INNER]), deps=[ld], inc=True)

    xT3 = xT.rearrange("(c p) s -> c p s", p=128)
    sz3 = sz_sp.rearrange("(c p) s -> c p s", p=128)
    tx_b = None
    t_sl = [None] * 16
    t_st = [None] * 16
    for t in range(NT):
        ld_x = em.dma("dsync", lambda e, t=t: e.dma_start(
            out=xe[:, :].rearrange("p (c s) -> p c s", s=TILE),
            in_=xT3[:, :, t * TILE:(t + 1) * TILE].rearrange("c p s -> p c s")),
            deps=[tx_b])
        tx_b = em.on("vector", lambda e: e.tensor_copy(xb[:, :], xe[:, :]),
                     deps=[ld_x, t_wup[-1], em.tok("tensor")], inc=True)
        for oc in range(16):
            so = oc % 2
            mmW = None
            for ic in range(8):
                mmW = em.on("tensor", lambda e, oc=oc, ic=ic, so=so: e.matmul(
                    pW[:, so * 512:so * 512 + TILE],
                    wup_bf[:, ic * INNER + oc * 128:ic * INNER + (oc + 1) * 128],
                    xb[:, ic * TILE:(ic + 1) * TILE],
                    start=(ic == 0), stop=(ic == 7)),
                    deps=([tx_b, t_sl[oc - 2]] if ic == 0 else []),
                    inc=(ic == 7))
            t_sl[oc] = em.on("scalar", lambda e, so=so: e.activation(
                szs[:, so * TILE:(so + 1) * TILE], pW[:, so * 512:so * 512 + TILE],
                AF.Silu), deps=[mmW, t_st[oc - 2]], inc=True)
            t_st[oc] = em.dma("dscalar", lambda e, oc=oc, t=t, so=so: e.dma_start(
                out=sz3[oc, :, t * TILE:(t + 1) * TILE],
                in_=szs[:, so * TILE:(so + 1) * TILE]), deps=[t_sl[oc]])




def phase_B(em, g, es):
    nc = em.nc
    q_sp = g.sread("qfm", (INNER, S), BF16)
    k_sp = g.sread("kfm", (INNER, S), BF16)
    v_sp = g.sread("vtm", (S, INNER), BF16)
    gates_sp = g.sread("gates8", (8, S), F32)
    xact_sp = g.sread("xact", (INNER, S), F32)
    sz_sp = g.sread("sz", (INNER, S), F32)
    biasig, biasfg = cinp(g, "biasig"), cinp(g, "biasfg")
    nw_c, sk_c = cinp(g, "nw"), cinp(g, "sk")
    ident = cinp(g, "ident")
    masks = cinp(g, "masks")
    hst_sp = g.swrite("hst", (INNER, S), BF16)
    grows = (g.swrite if DEBUG_B else g.scratch)("grows", (1, 8 * S), F32)
    if DEBUG_B:
        dbg = g.swrite("bdbg", (128, 1024 + 128 + 32 + 32), F32)
        dbg2 = g.swrite("bdbg2", (128, 2048 + 4096), F32)

    sb = lambda nm, shape, dt: es.enter_context(nc.sbuf_tensor(nm, shape, dt))
    H = S // 2
    igb = sb("b_igb", [4, H], F32)
    fgb = sb("b_fgb", [4, H], F32)
    r4a = sb("b_r4a", [4, H], F32)
    r4b = sb("b_r4b", [4, H], F32)
    ones4 = sb("b_ones4", [4, H], F32)
    Mrow = sb("b_Mrow", [4, H], F32)
    fcar = sb("b_fcar", [4, 1], F32)
    mcar = sb("b_mcar", [4, 1], F32)
    bgi = sb("b_bgi", [4, 1], F32)
    bgf = sb("b_bgf", [4, 1], F32)
    idsb = sb("b_id", [4, 4], F32)
    crow = sb("b_crow", [1, 32], F32)
    negc = sb("b_negc", [1, 32], F32)
    onesrow = sb("b_onesrow", [1, 128], F32)
    onescb = sb("b_onescb", [128, 1], BF16)
    onesc32 = sb("b_onesc32", [128, 1], F32)
    negbc = sb("b_negbc", [128, 32], F32)
    acols = sb("b_acols", [128, 128], F32)
    cexp = sb("b_cexp", [128, 1024], F32)
    msks = sb("b_msks", [128, 5 * 512], F32)
    nwsb = sb("b_nw", [128, 16], F32)
    sksb = sb("b_sk", [128, 16], F32)
    Mseg = sb("b_Mseg", [1, 512], F32)
    emdseg = sb("b_emdseg", [1, 512], F32)
    kb = sb("b_kb", [128, 4 * S], BF16)
    vb = sb("b_vb", [128, 32 * 512], BF16)
    qb = sb("b_qb", [128, 4 * 512], BF16)
    xasl = sb("b_xasl", [128, 4 * 512], F32)
    szsl = sb("b_szsl", [128, 4 * 512], F32)
    pbf = sb("b_pbf", [128, 4 * 512], BF16)
    hbuf = sb("b_hbuf", [128, 4 * 512], F32)
    tb2 = sb("b_tb2", [128, 512], F32)
    rows = sb("b_rows", [1, 8 * 512], F32)
    tbuf = sb("b_tbuf", [128, 512], F32)
    hsb = sb("b_hsb", [128, 4 * 512], BF16)

    pP = es.enter_context(nc.psum_tensor("b_pP", [128, 1024], F32))
    pH = es.enter_context(nc.psum_tensor("b_pH", [128, 2048], F32))
    pR = es.enter_context(nc.psum_tensor("b_pR", [1, 512], F32))

    ld3 = em.dma("dgp", lambda e: e.dma_start(out=bgi[:, :], in_=biasig))
    ld4 = em.dma("dgp", lambda e: e.dma_start(out=bgf[:, :], in_=biasfg))
    ld5 = em.dma("dgp", lambda e: e.dma_start(out=idsb[:, :], in_=ident[0:4, 0:4]))
    ld6 = em.dma("dsync", lambda e: e.dma_start(
        out=msks[:, :].rearrange("p (m s) -> p m s", s=512), in_=masks.rearrange("m p s -> p m s")))
    ld7 = em.dma("dsync", lambda e: e.dma_start(out=nwsb[:, :], in_=nw_c))
    ld8 = em.dma("dsync", lambda e: e.dma_start(out=sksb[:, :], in_=sk_c))
    em.on("vector", lambda e: e.memset(ones4[:, :], 1.0))
    t_on1 = em.on("vector", lambda e: e.memset(onesrow[:, :], 1.0), inc=True)
    em.on("vector", lambda e: e.memset(onescb[:, :], 1.0))
    t_on2 = em.on("vector", lambda e: e.memset(onesc32[:, :], 1.0), inc=True)
    gr8 = grows.rearrange("o (h s) -> (o h) s", s=S)
    ttr = None
    s1 = s2 = None
    tfc = tmc = None
    for ch in range(2):
        sl0 = ch * H
        ld1 = em.dma("dgp", lambda e, sl0=sl0: e.dma_start(
            out=igb[:, :], in_=gates_sp[0:4, sl0:sl0 + H]),
            deps=[ttr])
        ld2 = em.dma("dgp", lambda e, sl0=sl0: e.dma_start(
            out=fgb[:, :], in_=gates_sp[4:8, sl0:sl0 + H]), deps=[ttr])
        t_ig = em.on("vector", lambda e: e.tensor_scalar(
            igb[:, :], igb[:, :], bgi[:, 0:1], None, ALU.add), deps=[ld1, ld3], inc=True)
        t_fg = em.on("vector", lambda e: e.tensor_scalar(
            fgb[:, :], fgb[:, :], bgf[:, 0:1], None, ALU.add), deps=[ld2, ld4], inc=True)
        t1 = em.on("scalar", lambda e: e.activation(r4a[:, :], fgb[:, :], AF.Abs),
                   deps=[t_fg, s1], inc=True)
        t2 = em.on("scalar", lambda e: e.activation(r4a[:, :], r4a[:, :], AF.Exp, scale=-1.0),
                   deps=[t1], inc=True)
        t3 = em.on("scalar", lambda e: e.activation(r4a[:, :], r4a[:, :], AF.Ln, bias=1.0),
                   deps=[t2], inc=True)
        t4 = em.on("vector", lambda e: e.tensor_scalar_min(r4b[:, :], fgb[:, :], 0.0),
                   deps=[t_fg, s2], inc=True)
        t5 = em.on("vector", lambda e: e.tensor_sub(r4b[:, :], r4b[:, :], r4a[:, :]),
                   deps=[t3, t4], inc=True)
        t6 = em.on("vector", lambda e: e.tensor_tensor_scan(
            r4a[:, :], ones4[:, :], r4b[:, :], 0.0, ALU.mult, ALU.add),
            deps=[t5], inc=True)
        if ch == 1:
            t6 = em.on("vector", lambda e: e.tensor_scalar(
                r4a[:, :], r4a[:, :], fcar[:, 0:1], None, ALU.add),
                deps=[t6], hard=[tfc], inc=True)
        tfc = em.on("vector", lambda e: e.tensor_copy(fcar[:, :], r4a[:, H - 1:H]),
                    deps=[t6], hard=[t6], inc=True)
        t7 = em.on("vector", lambda e: e.tensor_sub(r4b[:, :], igb[:, :], r4a[:, :]),
                   deps=[t6, t_ig], inc=True)
        t8 = em.on("vector", lambda e: e.tensor_tensor_scan(
            Mrow[:, :], ones4[:, :], r4b[:, :], -1e30, ALU.mult, ALU.max),
            deps=[t7, s1], inc=True)
        if ch == 1:
            t8 = em.on("vector", lambda e: e.tensor_scalar(
                Mrow[:, :], Mrow[:, :], mcar[:, 0:1], None, ALU.max),
                deps=[t8], hard=[tmc], inc=True)
        tmc = em.on("vector", lambda e: e.tensor_copy(mcar[:, :], Mrow[:, H - 1:H]),
                    deps=[t8], hard=[t8], inc=True)
        for jc in range(16):
            ttr = em.on("tensor", lambda e, jc=jc, ch=ch: e.transpose(
                pP[:, 4 * (16 * ch + jc):4 * (16 * ch + jc) + 4],
                r4b[:, 128 * jc:128 * (jc + 1)], idsb[:, :]),
                deps=[t7, ld5] if jc == 0 else [], inc=(jc == 15))
        s1 = em.dma("dgp", lambda e, sl0=sl0: e.dma_start(
            out=gr8[0:4, sl0:sl0 + H], in_=Mrow[:, :]), deps=[t8])
        t9 = em.on("vector", lambda e: e.tensor_add(r4b[:, :], r4a[:, :], Mrow[:, :]),
                   deps=[t8, ttr], inc=True)
        t10 = em.on("scalar", lambda e: e.activation(r4b[:, :], r4b[:, :], AF.Exp, scale=-1.0),
                    deps=[t9], inc=True)
        s2 = em.dma("dgp", lambda e, sl0=sl0: e.dma_start(
            out=gr8[4:8, sl0:sl0 + H], in_=r4b[:, :]), deps=[t10])
    t_ac = em.on("vector", lambda e: e.tensor_copy(acols[:, :], pP[:, 0:128]),
                 deps=[ttr], inc=True)
    ld_cr = em.dma("dgp", lambda e: e.dma_start(
        out=crow[0:1, :].rearrange("o (c u) -> o c u", u=1),
        in_=grows[0:1, 0:4 * S].rearrange("o (c k) -> o c k", k=512)[:, :, 511:512],
        allow_slow_non_contiguous=True), deps=[s1])
    t_nc = em.on("vector", lambda e: e.tensor_scalar(
        negc[:, :], crow[:, :], -1.0, -LAM, ALU.mult, ALU.add), deps=[ld_cr], inc=True)
    mmb = em.on("tensor", lambda e: e.matmul(
        pP[:, 512:544], onesrow[:, :], negc[:, :], start=True, stop=True),
        deps=[t_nc, t_on1, t_ac], inc=True)
    t_nb = em.on("vector", lambda e: e.tensor_copy(negbc[:, :], pP[:, 512:544]),
                 deps=[mmb], inc=True)
    if DEBUG_B:
        em.dma("dscalar", lambda e: e.dma_start(out=dbg[:, 1024:1152], in_=acols[:, :]), deps=[t_ac])
        em.dma("dscalar", lambda e: e.dma_start(out=dbg[0:1, 1152:1184], in_=crow[:, :]), deps=[ld_cr])
        em.dma("dscalar", lambda e: e.dma_start(out=dbg[:, 1184:1216], in_=negbc[:, :]), deps=[t_nb])
    t_cx = [None] * 32
    for h in range(4):
        for i in range(8):
            col = 8 * h + i
            ahsl = acols[:, :].rearrange("p (jc hh) -> p hh jc", hh=4)[:, h, :]
            tsa = em.on("vector", lambda e, col=col, ahsl=ahsl: e.tensor_scalar(
                cexp[:, col * 32:(col + 1) * 32], ahsl,
                negbc[:, col:col + 1], 0.0, ALU.add, ALU.min),
                deps=[t_ac, t_nb], inc=True)
            t_cx[col] = em.on("scalar", lambda e, col=col: e.activation(
                cexp[:, col * 32:(col + 1) * 32], cexp[:, col * 32:(col + 1) * 32],
                AF.Exp), deps=[tsa], inc=True)

    if DEBUG_B:
        em.dma("dscalar", lambda e: e.dma_start(out=dbg[:, 0:1024], in_=cexp[:, :]), deps=[t_cx[31]])
    kfm3 = k_sp.rearrange("(c p) s -> c p s", p=128)
    qfm3 = q_sp.rearrange("(c p) s -> c p s", p=128)
    vtm3 = v_sp.rearrange("(j p) d -> j p d", p=128)
    xa4 = xact_sp.rearrange("(c p) s -> c p s", p=128)
    sz4 = sz_sp.rearrange("(c p) s -> c p s", p=128)
    hs4 = hst_sp.rearrange("(c p) s -> c p s", p=128)

    t_pbf = [t_ac, t_nb]
    t_rab = None
    t_rg = None
    t_hev = [None] * 4
    t_app = [None, None, None, mmb]
    t_sp = None
    for h in range(4):
        ld_k = em.dma("dsync", lambda e, h=h: e.dma_start(
            out=kb[:, :].rearrange("p (dc s) -> p dc s", s=S),
            in_=kfm3[4 * h:4 * h + 4, :, :].rearrange("c p s -> p c s")),
            deps=[em.tok("tensor")])
        ld_v = em.dma("dsync", lambda e, h=h: e.dma_start(
            out=vb[:, :].rearrange("p (j d) -> p j d", d=512),
            in_=vtm3[:, :, 512 * h:512 * (h + 1)].rearrange("j p d -> p j d")),
            deps=[em.tok("tensor")])
        for i in range(8):
            col = 8 * h + i
            ld_q = em.dma("dsync", lambda e, h=h, i=i: e.dma_start(
                out=qb[:, :].rearrange("p (dc s) -> p dc s", s=512),
                in_=qfm3[4 * h:4 * h + 4, :, 512 * i:512 * (i + 1)].rearrange("c p s -> p c s")),
                deps=[em.tok("tensor")])
            ld_xa = em.dma("dsync", lambda e, h=h, i=i: e.dma_start(
                out=xasl[:, :].rearrange("p (dc s) -> p dc s", s=512),
                in_=xa4[4 * h:4 * h + 4, :, 512 * i:512 * (i + 1)].rearrange("c p s -> p c s")),
                deps=[t_app[3]])
            ld_sz = em.dma("dsync", lambda e, h=h, i=i: e.dma_start(
                out=szsl[:, :].rearrange("p (dc s) -> p dc s", s=512),
                in_=sz4[4 * h:4 * h + 4, :, 512 * i:512 * (i + 1)].rearrange("c p s -> p c s")),
                deps=[t_app[3]])
            ld_M = em.dma("dgp", lambda e, col=col: e.dma_start(
                out=Mseg[:, :], in_=grows[0:1, col * 512:(col + 1) * 512]),
                deps=[s1, t_rg])
            ld_e = em.dma("dgp", lambda e, col=col: e.dma_start(
                out=emdseg[:, :], in_=grows[0:1, 4 * S + col * 512:4 * S + (col + 1) * 512]),
                deps=[s2, t_rg])
            nj = 4 * (i + 1)
            last_rs = None
            last_pv = None
            em.serial_now = ("j" in B_SER)
            for j in range(nj):
                sl = j % 2
                sb4 = j % 4
                mmp = None
                for dc in range(4):
                    mmp = em.on("tensor", lambda e, j=j, dc=dc, sl=sl: e.matmul(
                        pP[:, sl * 512:sl * 512 + 512],
                        kb[:, dc * S + j * 128:dc * S + (j + 1) * 128],
                        qb[:, dc * 512:(dc + 1) * 512],
                        start=(dc == 0), stop=(dc == 3)),
                        deps=([ld_k, ld_q, t_pbf[sl]] if dc == 0 else []),
                        inc=(dc == 3))
                m = (j - 4 * i) if j >= 4 * i else 4
                t_pbf[sl] = em.on("vector", lambda e, j=j, sb4=sb4, sl=sl, col=col, m=m: e.scalar_tensor_tensor(
                    pbf[:, sb4 * 512:(sb4 + 1) * 512], pP[:, sl * 512:sl * 512 + 512],
                    cexp[:, col * 32 + j:col * 32 + j + 1],
                    msks[:, m * 512:(m + 1) * 512], ALU.mult, ALU.mult),
                    deps=[mmp, t_cx[col], ld6], inc=True)
                last_rs = em.on("tensor", lambda e, j=j, sb4=sb4, nj=nj: e.matmul(
                    pR[:, :], onescb[:, 0:1], pbf[:, sb4 * 512:(sb4 + 1) * 512],
                    start=(j == 0), stop=(j == nj - 1)),
                    deps=[t_pbf[sl], t_on2] + ([t_rab] if j == 0 else []),
                    inc=(j == nj - 1))
                for dc in range(4):
                    last_pv = em.on("tensor", lambda e, j=j, dc=dc, sb4=sb4, nj=nj: e.matmul(
                        pH[:, dc * 512:(dc + 1) * 512],
                        vb[:, j * 512 + dc * 128:j * 512 + (dc + 1) * 128],
                        pbf[:, sb4 * 512:(sb4 + 1) * 512],
                        start=(j == 0), stop=(j == nj - 1)),
                        deps=([ld_v, t_app[3] if dc < 2 else t_hev[dc]] if j == 0 else []),
                        inc=(dc == 3 and j == nj - 1))
            em.serial_now = ("epi" in B_SER)
            mmu = mms = None
            for dc in range(4):
                t_hev[dc] = em.on("vector", lambda e, dc=dc: e.tensor_copy(
                    hbuf[:, dc * 512:(dc + 1) * 512], pH[:, dc * 512:(dc + 1) * 512]),
                    deps=[last_pv], inc=True)
                mmu = em.on("tensor", lambda e, dc=dc: e.matmul(
                    pP[0:1, 0:512], onesc32[:, 0:1], hbuf[:, dc * 512:(dc + 1) * 512],
                    start=(dc == 0), stop=(dc == 3)),
                    deps=[t_hev[dc]] + ([t_pbf[0]] if dc == 0 else []),
                    inc=(dc == 3))
                tsq = em.on("scalar", lambda e, dc=dc: e.activation(
                    tb2[:, :], hbuf[:, dc * 512:(dc + 1) * 512], AF.Square),
                    deps=[t_hev[dc], mms], inc=True)
                mms = em.on("tensor", lambda e, dc=dc: e.matmul(
                    pP[0:1, 512:1024], onesc32[:, 0:1], tb2[:, :],
                    start=(dc == 0), stop=(dc == 3)),
                    deps=[tsq] + ([t_pbf[1]] if dc == 0 else []),
                    inc=True)
            RO = lambda k2: rows[0:1, k2 * 512:(k2 + 1) * 512]
            r_mu = em.on("vector", lambda e: e.tensor_scalar(
                RO(0), pP[0:1, 0:512], 1.0 / 512, None, ALU.mult), deps=[mmu], inc=True)
            r_ms = em.on("vector", lambda e: e.tensor_scalar(
                RO(1), pP[0:1, 512:1024], 1.0 / 512, None, ALU.mult), deps=[mms], inc=True)
            t_pbf = [r_mu, r_ms]
            em.on("vector", lambda e: e.tensor_mul(RO(2), RO(0), RO(0)), deps=[r_mu])
            r_var = em.on("vector", lambda e: e.tensor_sub(RO(2), RO(1), RO(2)),
                          deps=[r_ms], inc=True)
            r_g = em.on("scalar", lambda e, col=col: e.activation(
                RO(3), Mseg[0:1, :], AF.Exp, scale=-1.0,
                bias=crow[0:1, col:col + 1]), deps=[ld_M, ld_cr, r_var], inc=True)
            t_rab = r_ab = em.on("scalar", lambda e: e.activation(
                RO(4), pR[0:1, :], AF.Abs), deps=[last_rs], inc=True)
            em.on("vector", lambda e: e.tensor_mul(RO(4), RO(4), RO(3)), deps=[r_g, r_ab])
            em.on("vector", lambda e: e.tensor_max(RO(4), RO(4), emdseg[0:1, :]),
                  deps=[ld_e])
            em.on("vector", lambda e: e.tensor_scalar(
                RO(4), RO(4), 1e-6, None, ALU.add))
            em.on("vector", lambda e: e.reciprocal(RO(4), RO(4)))
            r_phi = em.on("vector", lambda e: e.tensor_mul(RO(5), RO(3), RO(4)), inc=True)
            t_rg = r_phi
            em.on("vector", lambda e: e.tensor_mul(RO(6), RO(5), RO(5)))
            em.on("vector", lambda e: e.tensor_mul(RO(6), RO(6), RO(2)))
            r_u = em.on("vector", lambda e: e.tensor_scalar(
                RO(6), RO(6), 1e-6, 1e-30, ALU.add, ALU.max), inc=True)
            r_l = em.on("scalar", lambda e: e.activation(RO(6), RO(6), AF.Ln),
                        deps=[r_u], inc=True)
            r_rs = em.on("scalar", lambda e: e.activation(RO(7), RO(6), AF.Exp, scale=-0.5),
                         deps=[r_l], inc=True)
            r_sig = em.on("vector", lambda e: e.tensor_mul(RO(7), RO(7), RO(5)),
                          deps=[r_rs, r_phi], inc=True)
            mbc1 = em.on("tensor", lambda e: e.matmul(
                pH[:, 0:512], onesrow[:, :], RO(0), start=True, stop=True),
                deps=[r_mu, t_hev[3], t_on1], inc=True)
            mbc2 = em.on("tensor", lambda e: e.matmul(
                pH[:, 512:1024], onesrow[:, :], RO(7), start=True, stop=True),
                deps=[r_sig], inc=True)
            for dc in range(4):
                em.on("vector", lambda e, dc=dc: e.tensor_sub(
                    tbuf[:, :], hbuf[:, dc * 512:(dc + 1) * 512], pH[:, 0:512]),
                    deps=[mbc1])
                em.on("vector", lambda e: e.tensor_tensor(
                    tbuf[:, :], tbuf[:, :], pH[:, 512:1024], ALU.mult), deps=[mbc2])
                em.on("vector", lambda e, h=h, dc=dc: e.tensor_scalar(
                    tbuf[:, :], tbuf[:, :], nwsb[:, 4 * h + dc:4 * h + dc + 1], None,
                    ALU.mult), deps=[ld7])
                em.on("vector", lambda e, h=h, dc=dc: e.scalar_tensor_tensor(
                    tbuf[:, :], xasl[:, dc * 512:(dc + 1) * 512],
                    sksb[:, 4 * h + dc:4 * h + dc + 1], tbuf[:, :], ALU.mult, ALU.add),
                    deps=[ld_xa, ld8])
                t_app[dc] = em.on("vector", lambda e, dc=dc: e.tensor_tensor(
                    hsb[:, dc * 512:(dc + 1) * 512], tbuf[:, :],
                    szsl[:, dc * 512:(dc + 1) * 512], ALU.mult),
                    deps=[ld_sz, t_sp], inc=True)
            t_sp = em.dma("dscalar", lambda e, h=h, i=i: e.dma_start(
                out=hs4[4 * h:4 * h + 4, :, 512 * i:512 * (i + 1)].rearrange("c p s -> p c s"),
                in_=hsb[:, :].rearrange("p (dc s) -> p dc s", s=512)), deps=[t_app[3]])
            em.serial_now = False
            if DEBUG_B and (h, i) == DEBUG_B_TILE:
                em.dma("dscalar", lambda e: e.dma_start(out=dbg2[:, 0:2048], in_=hbuf[:, :]),
                       deps=[t_hev[3]])
                em.dma("dscalar", lambda e: e.dma_start(out=dbg2[0:1, 2048:2048 + 4096], in_=rows[:, :]),
                       deps=[r_sig, t_app[3]])




class LNAcc:

    def __init__(self, em, nch, tb, rows, pS, pB, onesc32, onesrow, eps, w=512):
        self.em = em
        self.w = w
        self.nch = nch
        self.tb = tb
        self.rows = rows
        self.pS = pS
        self.pB = pB
        self.onesc32 = onesc32
        self.onesrow = onesrow
        self.eps = eps
        self.k = 0
        self.mmu = None
        self.mms = None
        self.tsq = None
        self.war_mu = None
        self.war_ms = None
        self.war_b = [None, None]

    def add(self, rview, dep_r):
        em = self.em
        w = self.w
        first = (self.k == 0)
        last = (self.k == self.nch - 1)
        self.mmu = em.on("tensor", lambda e, rview=rview, w=w: e.matmul(
            self.pS[0:1, 0:w], self.onesc32[:, 0:1], rview,
            start=first, stop=last),
            deps=[dep_r] + ([self.war_mu] if first else []), inc=True)
        self.tsq = em.on("scalar", lambda e, rview=rview, w=w: e.activation(
            self.tb[:, 0:w], rview, AF.Square),
            deps=[dep_r, self.mms], inc=True)
        self.mms = em.on("tensor", lambda e, w=w: e.matmul(
            self.pS[0:1, 512:512 + w], self.onesc32[:, 0:1], self.tb[:, 0:w],
            start=first, stop=last),
            deps=[self.tsq] + ([self.war_ms] if first else []), inc=True)
        self.k += 1

    def finish(self):
        em = self.em
        w = self.w
        RO = lambda k2: self.rows[0:1, k2 * 512:k2 * 512 + w]
        n = 1.0 / (128 * self.nch)
        r_mu = em.on("vector", lambda e, n=n, w=w: e.tensor_scalar(
            RO(0), self.pS[0:1, 0:w], n, None, ALU.mult), deps=[self.mmu], inc=True)
        self.war_mu = r_mu
        r_ms = em.on("vector", lambda e, n=n, w=w: e.tensor_scalar(
            RO(1), self.pS[0:1, 512:512 + w], n, None, ALU.mult), deps=[self.mms], inc=True)
        self.war_ms = r_ms
        em.on("vector", lambda e: e.tensor_mul(RO(2), RO(0), RO(0)))
        em.on("vector", lambda e: e.tensor_sub(RO(2), RO(1), RO(2)))
        r_v = em.on("vector", lambda e: e.tensor_scalar(
            RO(2), RO(2), self.eps, 1e-30, ALU.add, ALU.max), inc=True)
        r_l = em.on("scalar", lambda e: e.activation(RO(2), RO(2), AF.Ln),
                    deps=[r_v], inc=True)
        r_rs = em.on("scalar", lambda e: e.activation(RO(3), RO(2), AF.Exp, scale=-0.5),
                     deps=[r_l], inc=True)
        mb1 = em.on("tensor", lambda e, w=w: e.matmul(
            self.pB[:, 0:w], self.onesrow[:, :], RO(0), start=True, stop=True),
            deps=[r_mu, self.war_b[0]], inc=True)
        mb2 = em.on("tensor", lambda e, w=w: e.matmul(
            self.pB[:, 512:512 + w], self.onesrow[:, :], RO(3), start=True, stop=True),
            deps=[r_rs, self.war_b[1]], inc=True)
        self.mb = (mb1, mb2)
        self.k = 0

    def apply(self, em, rview, dst32, dstbf, dep_extra=()):
        w = self.w
        t1 = em.on("vector", lambda e, rview=rview, dst32=dst32, w=w: e.tensor_sub(
            dst32, rview, self.pB[:, 0:w]), deps=[self.mb[0]] + list(dep_extra))
        t2 = em.on("vector", lambda e, dst32=dst32, w=w: e.tensor_tensor(
            dst32, dst32, self.pB[:, 512:512 + w], ALU.mult), deps=[self.mb[1]], inc=True)
        self.war_b = [t2, t2]
        if dstbf is None:
            return t2, None
        t3 = em.on("vector", lambda e, dst32=dst32, dstbf=dstbf: e.tensor_copy(
            dstbf, dst32), inc=True)
        return t2, t3


def _load_cast_weight(em, nm, src3, nchunks, cols, stage, dstbf, scale=None, lo_dst=None, after=None):
    toks = [None] * nchunks
    for k in range(nchunks):
        so = (k % 2) * cols
        ld = em.dma("dsync", lambda e, k=k, so=so: e.dma_start(
            out=stage[:, so:so + cols], in_=src3[k]),
            deps=[toks[k - 2]] if k >= 2 else [after])
        if scale is None:
            toks[k] = em.on("vector", lambda e, k=k, so=so: e.tensor_copy(
                dstbf[:, k * cols:(k + 1) * cols], stage[:, so:so + cols]),
                deps=[ld], inc=True)
            if lo_dst is not None:
                toks[k] = em.on("vector", lambda e, k=k, so=so: e.tensor_sub(
                    lo_dst[:, k * cols:(k + 1) * cols], stage[:, so:so + cols],
                    dstbf[:, k * cols:(k + 1) * cols]), deps=[toks[k]], inc=True)
        else:
            toks[k] = em.on("vector", lambda e, k=k, so=so: e.tensor_scalar(
                dstbf[:, k * cols:(k + 1) * cols], stage[:, so:so + cols],
                scale, None, ALU.mult), deps=[ld], inc=True)
    return toks



def phase_C(em, g, es):
    nc = em.nc
    xT = g.inp("xT", (D, S), F32)
    wdown = cinp(g, "wdown")
    hst_sp = g.sread("hst", (INNER, S), BF16)
    x1_sp = g.swrite("x1f", (D, S), F32)
    x1b_sp = g.swrite("x1b", (D, S), BF16)

    sb = lambda nm, shape, dt: es.enter_context(nc.sbuf_tensor(nm, shape, dt))
    wd_bf = sb("c_wd", [128, 16 * D], BF16)
    stage = sb("c_stage", [128, 2 * D], F32)
    hb = sb("c_hb", [128, 16 * TILE], BF16)
    xe = sb("c_xe", [128, 8 * TILE], F32)
    r32 = sb("c_r32", [128, 8 * TILE], F32)
    x1st = sb("c_x1", [128, 8 * TILE], F32)
    x1bs = sb("c_x1b", [128, 8 * TILE], BF16)
    tb = sb("c_tb", [128, TILE], F32)
    rows = sb("c_rows", [1, 4 * 512], F32)
    onesc32 = sb("c_on1", [128, 1], F32)
    onesrow = sb("c_on2", [1, 128], F32)
    pW = es.enter_context(nc.psum_tensor("c_pW", [128, 1024], F32))
    pS = es.enter_context(nc.psum_tensor("c_pS", [1, 1024], F32))
    pB = es.enter_context(nc.psum_tensor("c_pB", [128, 1024], F32))

    em.on("vector", lambda e: e.memset(onesc32[:, :], 1.0))
    t_on = em.on("vector", lambda e: e.memset(onesrow[:, :], 1.0), inc=True)
    wd3 = wdown.rearrange("(k p) o -> k p o", p=128)
    t_w = _load_cast_weight(em, "wd", wd3, 16, D, stage, wd_bf)
    ln = LNAcc(em, 8, tb, rows, pS, pB, onesc32, onesrow, 1e-5)
    ln.war_b = [t_on, t_on]
    hs3 = hst_sp.rearrange("(c p) s -> c p s", p=128)
    xT3 = xT.rearrange("(c p) s -> c p s", p=128)
    x13 = x1_sp.rearrange("(c p) s -> c p s", p=128)
    x1b3 = x1b_sp.rearrange("(c p) s -> c p s", p=128)
    t_evW = [None, None]
    t_radd = [None] * 8
    t_ap3 = None
    sp1 = sp2 = None
    ld_h = ld_x = None
    for t in range(NT):
        ld_h = em.dma("dsync", lambda e, t=t: e.dma_start(
            out=hb[:, :].rearrange("p (c s) -> p c s", s=TILE),
            in_=hs3[:, :, t * TILE:(t + 1) * TILE].rearrange("c p s -> p c s")),
            deps=[em.tok("tensor")])
        ld_x = em.dma("dsync", lambda e, t=t: e.dma_start(
            out=xe[:, :].rearrange("p (c s) -> p c s", s=TILE),
            in_=xT3[:, :, t * TILE:(t + 1) * TILE].rearrange("c p s -> p c s")),
            deps=[t_radd[7]])
        for oc in range(8):
            so = oc % 2
            mmW = None
            for ic in range(16):
                mmW = em.on("tensor", lambda e, oc=oc, ic=ic, so=so: e.matmul(
                    pW[:, so * 512:so * 512 + TILE],
                    wd_bf[:, ic * D + oc * 128:ic * D + (oc + 1) * 128],
                    hb[:, ic * TILE:(ic + 1) * TILE],
                    start=(ic == 0), stop=(ic == 15)),
                    deps=([ld_h, t_w[-1], t_evW[so]] if ic == 0 else []),
                    inc=(ic == 15))
            t_radd[oc] = em.on("vector", lambda e, oc=oc, so=so: e.tensor_add(
                r32[:, oc * TILE:(oc + 1) * TILE], pW[:, so * 512:so * 512 + TILE],
                xe[:, oc * TILE:(oc + 1) * TILE]), deps=[mmW, ld_x, t_ap3], inc=True)
            t_evW[so] = t_radd[oc]
            ln.add(r32[:, oc * TILE:(oc + 1) * TILE], t_radd[oc])
        ln.finish()
        for oc in range(8):
            dep_sp = [sp1, sp2] if oc < 2 else []
            t2, t3 = ln.apply(em, r32[:, oc * TILE:(oc + 1) * TILE],
                              x1st[:, oc * TILE:(oc + 1) * TILE],
                              x1bs[:, oc * TILE:(oc + 1) * TILE], dep_extra=dep_sp)
        t_ap3 = t3
        sp1 = em.dma("dscalar", lambda e, t=t: e.dma_start(
            out=x13[:, :, t * TILE:(t + 1) * TILE].rearrange("c p s -> p c s"),
            in_=x1st[:, :].rearrange("p (c s) -> p c s", s=TILE)), deps=[t3])
        sp2 = em.dma("dscalar", lambda e, t=t: e.dma_start(
            out=x1b3[:, :, t * TILE:(t + 1) * TILE].rearrange("c p s -> p c s"),
            in_=x1bs[:, :].rearrange("p (c s) -> p c s", s=TILE)), deps=[t3])



def phase_D1(em, g, es):
    nc = em.nc
    wgT, wuT = cinp(g, "wgT"), cinp(g, "wuT")
    x1b_sp = g.sread("x1b", (D, S), BF16)
    hid_sp = g.swrite("hid", (HID, S), BF16)

    sb = lambda nm, shape, dt: es.enter_context(nc.sbuf_tensor(nm, shape, dt))
    wg_bf = sb("d1_wg", [128, 8 * HID], BF16)
    wu_bf = sb("d1_wu", [128, 8 * HID], BF16)
    stage = sb("d1_stage", [128, 2 * HID], I32)
    xb = sb("d1_xb", [128, 8 * TILE], BF16)
    sg = sb("d1_sg", [128, TILE], BF16)
    hidb = sb("d1_hid", [128, 2 * 8 * TILE], BF16)
    pG = es.enter_context(nc.psum_tensor("d1_pG", [128, 1024], F32))
    pU = es.enter_context(nc.psum_tensor("d1_pU", [128, 1024], F32))

    wg3 = wgT.rearrange("(k p) o -> k p o", p=128)
    wu3 = wuT.rearrange("(k p) o -> k p o", p=128)
    t_wg = _load_cast_weight(em, "wg", wg3, 8, HID, stage, wg_bf, scale=1.0 / 127.0)
    t_wu = _load_cast_weight(em, "wu", wu3, 8, HID, stage, wu_bf, scale=1.0 / 127.0, after=t_wg[-1])
    x1b3 = x1b_sp.rearrange("(c p) s -> c p s", p=128)
    hid3 = hid_sp.rearrange("(c p) s -> c p s", p=128)
    tx = None
    t_evG = [None, None]
    t_evU = [None, None]
    t_mul = [None] * 24
    sp = [None, None, None]
    for t in range(NT):
        ld_x = em.dma("dsync", lambda e, t=t: e.dma_start(
            out=xb[:, :].rearrange("p (c s) -> p c s", s=TILE),
            in_=x1b3[:, :, t * TILE:(t + 1) * TILE].rearrange("c p s -> p c s")),
            deps=[em.tok("tensor")])
        for oc in range(24):
            so = oc % 2
            grp = (oc // 8) % 2
            mmG = mmU = None
            for ic in range(8):
                mmG = em.on("tensor", lambda e, oc=oc, ic=ic, so=so: e.matmul(
                    pG[:, so * 512:so * 512 + TILE],
                    wg_bf[:, ic * HID + oc * 128:ic * HID + (oc + 1) * 128],
                    xb[:, ic * TILE:(ic + 1) * TILE],
                    start=(ic == 0), stop=(ic == 7)),
                    deps=([ld_x, t_wg[-1], t_evG[so]] if ic == 0 else []),
                    inc=(ic == 7))
            for ic in range(8):
                mmU = em.on("tensor", lambda e, oc=oc, ic=ic, so=so: e.matmul(
                    pU[:, so * 512:so * 512 + TILE],
                    wu_bf[:, ic * HID + oc * 128:ic * HID + (oc + 1) * 128],
                    xb[:, ic * TILE:(ic + 1) * TILE],
                    start=(ic == 0), stop=(ic == 7)),
                    deps=([t_wu[-1], t_evU[so]] if ic == 0 else []),
                    inc=(ic == 7))
            t_sl = em.on("scalar", lambda e, so=so: e.activation(
                sg[:, :], pG[:, so * 512:so * 512 + TILE], AF.Silu),
                deps=[mmG, t_mul[oc - 1]], inc=True)
            t_evG[so] = t_sl
            t_mul[oc] = em.on("vector", lambda e, oc=oc, so=so, grp=grp: e.tensor_tensor(
                hidb[:, grp * 8 * TILE + (oc % 8) * TILE:grp * 8 * TILE + (oc % 8 + 1) * TILE],
                sg[:, :], pU[:, so * 512:so * 512 + TILE], ALU.mult),
                deps=[t_sl, mmU, sp[(oc // 8 + 1) % 3] if oc % 8 == 0 else None],
                inc=True)
            t_evU[so] = t_mul[oc]
            if oc % 8 == 7:
                sp[oc // 8] = em.dma("dscalar", lambda e, t=t, oc=oc, grp=grp: e.dma_start(
                    out=hid3[(oc // 8) * 8:(oc // 8) * 8 + 8, :, t * TILE:(t + 1) * TILE].rearrange("c p s -> p c s"),
                    in_=hidb[:, grp * 8 * TILE:(grp + 1) * 8 * TILE].rearrange("p (c s) -> p c s", s=TILE)),
                    deps=[t_mul[oc]])



def phase_D2(em, g, es):
    nc = em.nc
    wdT = cinp(g, "wdT")
    hid_sp = g.sread("hid", (HID, S), BF16)
    x1_sp = g.sread("x1f", (D, S), F32)
    x2_sp = g.swrite("x2f", (D, S), F32)
    x2b_sp = g.swrite("x2b", (D, S), BF16)

    sb = lambda nm, shape, dt: es.enter_context(nc.sbuf_tensor(nm, shape, dt))
    wd_bf = sb("d2_wd", [128, 24 * D], BF16)
    stage = sb("d2_stage", [128, 2 * D], I32)
    hb = sb("d2_hb", [128, 24 * TILE], BF16)
    xe = sb("d2_xe", [128, 8 * TILE], F32)
    r32 = sb("d2_r32", [128, 8 * TILE], F32)
    x2st = sb("d2_x2", [128, 8 * TILE], F32)
    x2bs = sb("d2_x2b", [128, 8 * TILE], BF16)
    tb = sb("d2_tb", [128, TILE], F32)
    rows = sb("d2_rows", [1, 4 * 512], F32)
    onesc32 = sb("d2_on1", [128, 1], F32)
    onesrow = sb("d2_on2", [1, 128], F32)
    pW = es.enter_context(nc.psum_tensor("d2_pW", [128, 1024], F32))
    pS = es.enter_context(nc.psum_tensor("d2_pS", [1, 1024], F32))
    pB = es.enter_context(nc.psum_tensor("d2_pB", [128, 1024], F32))

    em.on("vector", lambda e: e.memset(onesc32[:, :], 1.0))
    t_on = em.on("vector", lambda e: e.memset(onesrow[:, :], 1.0), inc=True)
    wd3 = wdT.rearrange("(k p) o -> k p o", p=128)
    t_w = _load_cast_weight(em, "wd2", wd3, 24, D, stage, wd_bf, scale=1.0 / 127.0)
    ln = LNAcc(em, 8, tb, rows, pS, pB, onesc32, onesrow, 1e-5)
    ln.war_b = [t_on, t_on]
    hs3 = hid_sp.rearrange("(c p) s -> c p s", p=128)
    x13 = x1_sp.rearrange("(c p) s -> c p s", p=128)
    x23 = x2_sp.rearrange("(c p) s -> c p s", p=128)
    x2b3 = x2b_sp.rearrange("(c p) s -> c p s", p=128)
    t_evW = [None, None]
    t_radd = [None] * 8
    t_ap3 = None
    sp1 = sp2 = None
    for t in range(NT):
        ld_h = em.dma("dsync", lambda e, t=t: e.dma_start(
            out=hb[:, :].rearrange("p (c s) -> p c s", s=TILE),
            in_=hs3[:, :, t * TILE:(t + 1) * TILE].rearrange("c p s -> p c s")),
            deps=[em.tok("tensor")])
        ld_x = em.dma("dsync", lambda e, t=t: e.dma_start(
            out=xe[:, :].rearrange("p (c s) -> p c s", s=TILE),
            in_=x13[:, :, t * TILE:(t + 1) * TILE].rearrange("c p s -> p c s")),
            deps=[t_radd[7]])
        for oc in range(8):
            so = oc % 2
            mmW = None
            for ic in range(24):
                mmW = em.on("tensor", lambda e, oc=oc, ic=ic, so=so: e.matmul(
                    pW[:, so * 512:so * 512 + TILE],
                    wd_bf[:, ic * D + oc * 128:ic * D + (oc + 1) * 128],
                    hb[:, ic * TILE:(ic + 1) * TILE],
                    start=(ic == 0), stop=(ic == 23)),
                    deps=([ld_h, t_w[-1], t_evW[so]] if ic == 0 else []),
                    inc=(ic == 23))
            t_radd[oc] = em.on("vector", lambda e, oc=oc, so=so: e.tensor_add(
                r32[:, oc * TILE:(oc + 1) * TILE], pW[:, so * 512:so * 512 + TILE],
                xe[:, oc * TILE:(oc + 1) * TILE]), deps=[mmW, ld_x, t_ap3], inc=True)
            t_evW[so] = t_radd[oc]
            ln.add(r32[:, oc * TILE:(oc + 1) * TILE], t_radd[oc])
        ln.finish()
        for oc in range(8):
            dep_sp = [sp1, sp2] if oc < 2 else []
            t2, t3 = ln.apply(em, r32[:, oc * TILE:(oc + 1) * TILE],
                              x2st[:, oc * TILE:(oc + 1) * TILE],
                              x2bs[:, oc * TILE:(oc + 1) * TILE], dep_extra=dep_sp)
        t_ap3 = t3
        sp1 = em.dma("dscalar", lambda e, t=t: e.dma_start(
            out=x23[:, :, t * TILE:(t + 1) * TILE].rearrange("c p s -> p c s"),
            in_=x2st[:, :].rearrange("p (c s) -> p c s", s=TILE)), deps=[t3])
        sp2 = em.dma("dscalar", lambda e, t=t: e.dma_start(
            out=x2b3[:, :, t * TILE:(t + 1) * TILE].rearrange("c p s -> p c s"),
            in_=x2bs[:, :].rearrange("p (c s) -> p c s", s=TILE)), deps=[t3])



def phase_E(em, g, es):
    nc = em.nc
    iwT, projT = cinp(g, "iwT"), cinp(g, "projT")
    xisrow, xiscols = cinp(g, "xisrow"), cinp(g, "xiscols")
    x2_sp = g.sread("x2f", (D, S), F32)
    x2b_sp = g.sread("x2b", (D, S), BF16)
    x3_sp = g.swrite("x3f", (D, S), F32)
    x3b_sp = g.swrite("x3b", (D, S), BF16)

    sb = lambda nm, shape, dt: es.enter_context(nc.sbuf_tensor(nm, shape, dt))
    pj_bf = sb("e_pj", [128, 8 * R], BF16)
    pj_lo = sb("e_pjlo", [128, 8 * R], BF16)
    iw_bf = sb("e_iw", [128, 8 * D], BF16)
    iw_lo = sb("e_iwlo", [128, 8 * D], BF16)
    stage = sb("e_stage", [128, 2 * R], F32)
    wrfs = sb("e_wr", [128, 8 * D], BF16)
    xrow = sb("e_xr", [1, R], F32)
    urow = sb("e_ur", [1, R], F32)
    xcol = sb("e_xc", [128, 8], F32)
    bcol = sb("e_bc", [128, 8], F32)
    niw = sb("e_niw", [1, R], F32)
    nx = sb("e_nx", [1, 512], F32)
    nsq = sb("e_nsq", [128, R], F32)
    xe = sb("e_xe", [128, 8 * TILE], F32)
    xb = sb("e_xb", [128, 8 * TILE], BF16)
    xlo = sb("e_xlo", [128, 8 * TILE], BF16)
    feats = sb("e_ft", [128, 8 * TILE], BF16)
    r32 = sb("e_r32", [128, 8 * TILE], F32)
    x3st = sb("e_x3", [128, 8 * TILE], F32)
    x3bs = sb("e_x3b", [128, 8 * TILE], BF16)
    tb = sb("e_tb", [128, TILE], F32)
    rows = sb("e_rows", [1, 4 * 512], F32)
    onesc32 = sb("e_on1", [128, 1], F32)
    onesrow = sb("e_on2", [1, 128], F32)
    pD = es.enter_context(nc.psum_tensor("e_pD", [128, 1024], F32))
    pW = es.enter_context(nc.psum_tensor("e_pW", [128, 1024], F32))
    pS = es.enter_context(nc.psum_tensor("e_pS", [1, 1024], F32))
    pB = es.enter_context(nc.psum_tensor("e_pB", [128, 1024], F32))

    em.on("vector", lambda e: e.memset(onesc32[:, :], 1.0))
    t_on = em.on("vector", lambda e: e.memset(onesrow[:, :], 1.0), inc=True)
    pj3 = projT.rearrange("(k p) r -> k p r", p=128)
    iw3 = iwT.rearrange("(k p) o -> k p o", p=128)
    t_pj = _load_cast_weight(em, "pj", pj3, 8, R, stage, pj_bf, lo_dst=pj_lo)
    t_iw = _load_cast_weight(em, "iw", iw3, 8, D, stage, iw_bf, lo_dst=iw_lo, after=t_pj[-1])
    ld_xr = em.dma("dgp", lambda e: e.dma_start(out=xrow[:, :], in_=xisrow))
    ld_xc = em.dma("dsync", lambda e: e.dma_start(out=xcol[:, :], in_=xiscols))
    t_ur = em.on("vector", lambda e: e.tensor_scalar(
        urow[:, :], xrow[:, :], -0.5, None, ALU.mult), deps=[ld_xr], inc=True)
    t_bc = em.on("vector", lambda e: e.tensor_scalar(
        bcol[:, :], xcol[:, :], float(np.log(1.0 / 32.0)), None, ALU.add),
        deps=[ld_xc], inc=True)
    t_sq = None
    mmn = None
    for k in range(8):
        em.on("vector", lambda e, k=k: e.scalar_tensor_tensor(
            nsq[:, :], iw_lo[:, k * D:(k + 1) * D], 2.0,
            iw_bf[:, k * D:(k + 1) * D], ALU.mult, ALU.add),
            deps=[t_iw[k], mmn])
        t_sq = em.on("vector", lambda e, k=k: e.tensor_tensor(
            nsq[:, :], nsq[:, :], iw_bf[:, k * D:(k + 1) * D], ALU.mult),
            inc=True)
        for half in range(2):
            mmn = em.on("tensor", lambda e, k=k, half=half: e.matmul(
                pS[0:1, half * 512:(half + 1) * 512], onesc32[:, 0:1],
                nsq[:, half * 512:half * 512 + 512],
                start=(k == 0), stop=(k == 7)), deps=[t_sq], inc=True)
    t_niw = em.on("vector", lambda e: e.tensor_copy(niw[0:1, 0:512], pS[0:1, 0:512]),
                  deps=[mmn], inc=True)
    t_niw = em.on("vector", lambda e: e.tensor_copy(niw[0:1, 512:1024], pS[0:1, 512:1024]),
                  deps=[mmn], inc=True)
    t_wr = None
    t_evD = [t_niw, t_niw]
    for rc in range(8):
        for ot in range(2):
            sl = (2 * rc + ot) % 2
            mmD = None
            for pp, (lhs_, rhs_) in enumerate(((pj_bf, iw_bf), (pj_bf, iw_lo), (pj_lo, iw_bf))):
                for ic in range(8):
                    mmD = em.on("tensor", lambda e, rc=rc, ot=ot, ic=ic, sl=sl, pp=pp, lhs_=lhs_, rhs_=rhs_: e.matmul(
                        pD[:, sl * 512:sl * 512 + 512],
                        lhs_[:, ic * R + rc * 128:ic * R + (rc + 1) * 128],
                        rhs_[:, ic * D + ot * 512:ic * D + ot * 512 + 512],
                        start=(pp == 0 and ic == 0), stop=False),
                        deps=([t_pj[-1], t_iw[-1], t_evD[sl]] if pp == 0 and ic == 0 else []))
            mmD = em.on("tensor", lambda e, rc=rc, ot=ot, sl=sl: e.matmul(
                pD[:, sl * 512:sl * 512 + 512], urow[0:1, rc * 128:(rc + 1) * 128],
                niw[0:1, ot * 512:(ot + 1) * 512], start=False, stop=True),
                deps=[t_ur, t_niw], inc=True)
            t_wr = em.on("scalar", lambda e, rc=rc, ot=ot, sl=sl: e.activation(
                wrfs[:, rc * D + ot * 512:rc * D + ot * 512 + 512],
                pD[:, sl * 512:sl * 512 + 512], AF.Exp,
                bias=bcol[:, rc:rc + 1], scale=xcol[:, rc:rc + 1]),
                deps=[mmD, t_bc, ld_xc], inc=True)
            t_evD[sl] = t_wr

    ln = LNAcc(em, 8, tb, rows, pS, pB, onesc32, onesrow, 1e-5)
    ln.war_b = [t_on, t_on]
    ln.war_mu = mmn
    ln.war_ms = mmn
    x23 = x2_sp.rearrange("(c p) s -> c p s", p=128)
    x2b3 = x2b_sp.rearrange("(c p) s -> c p s", p=128)
    x33 = x3_sp.rearrange("(c p) s -> c p s", p=128)
    x3b3 = x3b_sp.rearrange("(c p) s -> c p s", p=128)
    t_radd = [None] * 8
    t_ap3 = None
    t_nx = t_wr
    t_ft = [t_wr] * 8
    t_evW = [None, None]
    sp1 = sp2 = None
    for t in range(NT):
        ld_x = em.dma("dsync", lambda e, t=t: e.dma_start(
            out=xe[:, :].rearrange("p (c s) -> p c s", s=TILE),
            in_=x23[:, :, t * TILE:(t + 1) * TILE].rearrange("c p s -> p c s")),
            deps=[t_radd[7], t_nx])
        ld_xb = em.dma("dsync", lambda e, t=t: e.dma_start(
            out=xb[:, :].rearrange("p (c s) -> p c s", s=TILE),
            in_=x2b3[:, :, t * TILE:(t + 1) * TILE].rearrange("c p s -> p c s")),
            deps=[em.tok("tensor")])
        t_xlo = em.on("vector", lambda e: e.tensor_sub(xlo[:, :], xe[:, :], xb[:, :]),
                      deps=[ld_x, ld_xb], inc=True)
        mmx = None
        t_sx = None
        for k in range(8):
            t_sx = em.on("scalar", lambda e, k=k: e.activation(
                tb[:, :], xe[:, k * TILE:(k + 1) * TILE], AF.Square),
                deps=[ld_x, mmx], inc=True)
            mmx = em.on("tensor", lambda e, k=k: e.matmul(
                pS[0:1, 0:512], onesc32[:, 0:1], tb[:, :],
                start=(k == 0), stop=(k == 7)),
                deps=[t_sx] + ([ln.war_mu] if k == 0 else []), inc=True)
        ln.war_mu = None
        t_nx = em.on("vector", lambda e: e.tensor_scalar(
            nx[:, :], pS[0:1, 0:512], 1.0, None, ALU.mult), deps=[mmx], inc=True)
        for rc in range(8):
            sl = rc % 2
            mmD = None
            for pp, (lhs_, rhs_) in enumerate(((pj_bf, xb), (pj_bf, xlo), (pj_lo, xb))):
                for ic in range(8):
                    mmD = em.on("tensor", lambda e, rc=rc, ic=ic, sl=sl, pp=pp, lhs_=lhs_, rhs_=rhs_: e.matmul(
                        pD[:, sl * 512:sl * 512 + 512],
                        lhs_[:, ic * R + rc * 128:ic * R + (rc + 1) * 128],
                        rhs_[:, ic * TILE:(ic + 1) * TILE],
                        start=(pp == 0 and ic == 0), stop=False),
                        deps=([ld_xb, t_xlo, t_ft[rc - 2]] if pp == 0 and ic == 0 else []))
            mmD = em.on("tensor", lambda e, rc=rc, sl=sl: e.matmul(
                pD[:, sl * 512:sl * 512 + 512], urow[0:1, rc * 128:(rc + 1) * 128],
                nx[0:1, 0:512], start=False, stop=True), deps=[t_nx], inc=True)
            t_ft[rc] = em.on("scalar", lambda e, rc=rc, sl=sl: e.activation(
                feats[:, rc * TILE:(rc + 1) * TILE], pD[:, sl * 512:sl * 512 + 512],
                AF.Exp, bias=bcol[:, rc:rc + 1], scale=xcol[:, rc:rc + 1]),
                deps=[mmD], inc=True)
        for oc in range(8):
            so = oc % 2
            mmW = None
            for rc in range(8):
                mmW = em.on("tensor", lambda e, oc=oc, rc=rc, so=so: e.matmul(
                    pW[:, so * 512:so * 512 + TILE],
                    wrfs[:, rc * D + oc * 128:rc * D + (oc + 1) * 128],
                    feats[:, rc * TILE:(rc + 1) * TILE],
                    start=(rc == 0), stop=(rc == 7)),
                    deps=([t_ft[7], t_evW[so]] if rc == 0 else []),
                    inc=(rc == 7))
            t_radd[oc] = em.on("vector", lambda e, oc=oc, so=so: e.tensor_add(
                r32[:, oc * TILE:(oc + 1) * TILE], pW[:, so * 512:so * 512 + TILE],
                xe[:, oc * TILE:(oc + 1) * TILE]), deps=[mmW, t_ap3], inc=True)
            t_evW[so] = t_radd[oc]
            ln.add(r32[:, oc * TILE:(oc + 1) * TILE], t_radd[oc])
        ln.finish()
        for oc in range(8):
            dep_sp = [sp1, sp2] if oc < 2 else []
            t2, t3 = ln.apply(em, r32[:, oc * TILE:(oc + 1) * TILE],
                              x3st[:, oc * TILE:(oc + 1) * TILE],
                              x3bs[:, oc * TILE:(oc + 1) * TILE], dep_extra=dep_sp)
        t_ap3 = t3
        sp1 = em.dma("dscalar", lambda e, t=t: e.dma_start(
            out=x33[:, :, t * TILE:(t + 1) * TILE].rearrange("c p s -> p c s"),
            in_=x3st[:, :].rearrange("p (c s) -> p c s", s=TILE)), deps=[t3])
        sp2 = em.dma("dscalar", lambda e, t=t: e.dma_start(
            out=x3b3[:, :, t * TILE:(t + 1) * TILE].rearrange("c p s -> p c s"),
            in_=x3bs[:, :].rearrange("p (c s) -> p c s", s=TILE)), deps=[t3])



TILE_F = 256
NT_F = S // TILE_F

def phase_F(em, g, es):
    nc = em.nc
    rmblk, rupblk, cpT = cinp(g, "rmblk"), cinp(g, "rupblk"), cinp(g, "cpT")
    x3_sp = g.sread("x3f", (D, S), F32)
    x3b_sp = g.sread("x3b", (D, S), BF16)
    out_sp = g.out("outT", (D, S), F32)
    cp_bf = g.scratch("cpbf16", (GH, D), BF16)

    sb = lambda nm, shape, dt: es.enter_context(nc.sbuf_tensor(nm, shape, dt))
    TF = TILE_F
    rm_bf = sb("f_rm", [128, 32 * 128], BF16)
    rup_bf = sb("f_rup", [128, 128 * 128], BF16)
    stage = sb("f_stage", [128, 2 * 4096], F32)
    stbf = sb("f_stbf", [128, 2 * 4096], BF16)
    ghb = sb("f_gh", [128, 128 * TF], BF16)
    cpb = sb("f_cpb", [128, 2 * 8 * 512], BF16)
    ybuf = sb("f_y", [128, 2 * TF], BF16)
    xb = sb("f_xb", [128, 8 * TF], BF16)
    xe = sb("f_xe", [128, 8 * TF], F32)
    r32 = sb("f_r32", [128, 8 * TF], F32)
    outst = sb("f_out", [128, 8 * TF], F32)
    tb = sb("f_tb", [128, TF], F32)
    rows = sb("f_rows", [1, 4 * 512], F32)
    onesc32 = sb("f_on1", [128, 1], F32)
    onesrow = sb("f_on2", [1, 128], F32)
    pYZ = es.enter_context(nc.psum_tensor("f_pYZ", [128, 2048], F32))
    pC = es.enter_context(nc.psum_tensor("f_pC", [128, 2048], F32))

    em.on("vector", lambda e: e.memset(onesc32[:, :], 1.0))
    t_on = em.on("vector", lambda e: e.memset(onesrow[:, :], 1.0), inc=True)
    rm3 = rmblk.rearrange("h c p n -> (h c) p n")
    t_rm = _load_cast_weight(em, "rm", rm3, 32, 128, stage, rm_bf)
    rup3 = rupblk.rearrange("h c j p n -> (h c j) p n")
    t_rup = _load_cast_weight(em, "rup", rup3, 128, 128, stage, rup_bf, after=t_rm[-1])
    cp3 = cpT.rearrange("(g p) o -> g p o", p=128)
    cpb3 = cp_bf.rearrange("(g p) o -> g p o", p=128)
    st_c = [None, None]
    tc_c = [None, None]
    for grp in range(32):
        b = grp % 2
        ldc = em.dma("dsync", lambda e, grp=grp, b=b: e.dma_start(
            out=stage[:, b * 4096:(b + 1) * 4096].rearrange("p (c o) -> p c o", o=D),
            in_=cp3[4 * grp:4 * grp + 4].rearrange("c p o -> p c o")),
            deps=[tc_c[b], t_rup[-1] if grp < 2 else None])
        tc_c[b] = em.on("vector", lambda e, b=b: e.tensor_copy(
            stbf[:, b * 4096:(b + 1) * 4096], stage[:, b * 4096:(b + 1) * 4096]),
            deps=[ldc, st_c[b]], inc=True)
        st_c[b] = em.dma("dscalar", lambda e, grp=grp, b=b: e.dma_start(
            out=cpb3[4 * grp:4 * grp + 4].rearrange("c p o -> p c o"),
            in_=stbf[:, b * 4096:(b + 1) * 4096].rearrange("p (c o) -> p c o", o=D)),
            deps=[tc_c[b]])

    ln = LNAcc(em, 8, tb, rows, pYZ, pYZ[:, 1024:2048], onesc32, onesrow, 1e-5, w=TF)
    ln.war_b = [t_on, t_on]
    x33 = x3_sp.rearrange("(c p) s -> c p s", p=128)
    x3b3 = x3b_sp.rearrange("(c p) s -> c p s", p=128)
    o3 = out_sp.rearrange("(c p) s -> c p s", p=128)
    TFW = TF
    t_gel = [None, None]
    t_evY = [None, None]
    t_mmZ = [None, None]
    t_cpmm = [None, None]
    t_radd = [None] * 8
    t_ap = None
    t_ghlast = None
    sp_out = None
    for t in range(NT_F):
        ld_xb = em.dma("dsync", lambda e, t=t: e.dma_start(
            out=xb[:, :].rearrange("p (c s) -> p c s", s=TF),
            in_=x3b3[:, :, t * TF:(t + 1) * TF].rearrange("c p s -> p c s")),
            deps=[em.tok("tensor")])
        ld_x = em.dma("dsync", lambda e, t=t: e.dma_start(
            out=xe[:, :].rearrange("p (c s) -> p c s", s=TF),
            in_=x33[:, :, t * TF:(t + 1) * TF].rearrange("c p s -> p c s")),
            deps=[t_radd[7]])
        kidx = 0
        for h in range(4):
            for ic in range(8):
                p_ = kidx % 2
                mmY = em.on("tensor", lambda e, h=h, ic=ic, p_=p_: e.matmul(
                    pYZ[:, p_ * 512:p_ * 512 + TFW],
                    rm_bf[:, (h * 8 + ic) * 128:(h * 8 + ic + 1) * 128],
                    xb[:, ic * TF:(ic + 1) * TF], start=True, stop=True),
                    deps=[ld_xb, t_rm[-1], t_evY[p_]] +
                         ([ln.war_mu, ln.war_ms] if kidx == 0 else []), inc=True)
                t_evY[p_] = em.on("vector", lambda e, p_=p_: e.tensor_copy(
                    ybuf[:, p_ * TF:(p_ + 1) * TF], pYZ[:, p_ * 512:p_ * 512 + TFW]),
                    deps=[mmY, t_mmZ[p_]], inc=True)
                for js in range(4):
                    zp = js % 2
                    t_mmZ[p_] = em.on("tensor", lambda e, h=h, ic=ic, js=js, p_=p_, zp=zp: e.matmul(
                        pYZ[:, 1024 + zp * 512:1024 + zp * 512 + TFW],
                        rup_bf[:, ((h * 8 + ic) * 4 + js) * 128:((h * 8 + ic) * 4 + js + 1) * 128],
                        ybuf[:, p_ * TF:(p_ + 1) * TF], start=True, stop=True),
                        deps=[t_evY[p_], t_rup[-1], t_gel[zp]] +
                             ([t_ap] if kidx == 0 and js < 2 else []), inc=True)
                    kc = (h * 8 + ic) * 4 + js
                    t_gel[zp] = em.on("scalar", lambda e, kc=kc, zp=zp: e.activation(
                        ghb[:, kc * TF:(kc + 1) * TF],
                        pYZ[:, 1024 + zp * 512:1024 + zp * 512 + TFW], AF.Gelu),
                        deps=[t_mmZ[p_], t_ghlast if kc < 2 else None], inc=True)
                kidx += 1
        t_ghfirst = t_gel[0]
        t_evC = [None] * 4
        for half in range(2):
            lds = [None, None]
            mmlast = None
            for g16 in range(16):
                b = g16 % 2
                lds[b] = em.dma("dsync", lambda e, g16=g16, b=b, half=half: e.dma_start(
                    out=cpb[:, b * 4096:(b + 1) * 4096].rearrange("p (c o) -> p c o", o=512),
                    in_=cpb3[8 * g16:8 * g16 + 8, :, half * 512:(half + 1) * 512].rearrange("c p o -> p c o")),
                    deps=[t_cpmm[b], st_c[0], st_c[1]])
                for kcg in range(8):
                    kc = g16 * 8 + kcg
                    for oc4 in range(4):
                        mmlast = em.on("tensor", lambda e, b=b, kcg=kcg, oc4=oc4, kc=kc: e.matmul(
                            pC[:, oc4 * 512:oc4 * 512 + TFW],
                            cpb[:, b * 4096 + kcg * 512 + oc4 * 128:b * 4096 + kcg * 512 + (oc4 + 1) * 128],
                            ghb[:, kc * TF:(kc + 1) * TF],
                            start=(kc == 0), stop=(kc == 127)),
                            deps=([lds[b]] if kcg == 0 and oc4 == 0 else []) +
                                 ([t_gel[0], t_gel[1]] if kc == 0 and oc4 == 0 else []) +
                                 ([t_evC[oc4]] if kc == 0 else []), inc=(kcg == 7 and oc4 == 3))
                t_cpmm[b] = mmlast
            for oc4 in range(4):
                oc = half * 4 + oc4
                t_radd[oc] = em.on("vector", lambda e, oc=oc, oc4=oc4: e.tensor_add(
                    r32[:, oc * TF:(oc + 1) * TF], pC[:, oc4 * 512:oc4 * 512 + TFW],
                    xe[:, oc * TF:(oc + 1) * TF]), deps=[mmlast, ld_x, t_ap], inc=True)
                t_evC[oc4] = t_radd[oc]
                ln.add(r32[:, oc * TF:(oc + 1) * TF], t_radd[oc])
        ln.finish()
        for oc in range(8):
            t_ap, _ = ln.apply(em, r32[:, oc * TF:(oc + 1) * TF],
                               outst[:, oc * TF:(oc + 1) * TF], None,
                               dep_extra=[sp_out] if oc < 2 else [])
        t_ghlast = t_ap
        sp_out = em.dma("dscalar", lambda e, t=t: e.dma_start(
            out=o3[:, :, t * TF:(t + 1) * TF].rearrange("c p s -> p c s"),
            in_=outst[:, :].rearrange("p (c s) -> p c s", s=TF)), deps=[t_ap])
PHASE_FNS = {}
PHASE_FNS["A1"] = phase_A1
PHASE_FNS["A2"] = phase_A2
PHASE_FNS["B"] = phase_B
PHASE_FNS["C"] = phase_C
PHASE_FNS["D1"] = phase_D1
PHASE_FNS["D2"] = phase_D2
PHASE_FNS["E"] = phase_E
PHASE_FNS["F"] = phase_F
PHASE_ORDER = ["A1", "A2", "B", "C", "D1", "D2", "E", "F"]



def _build(phases, fused):
    nc = bass.Bass("TRN2", target_bir_lowering=False)
    es = ExitStack()
    g = G(nc, fused)
    em = Emit(nc, es)
    for i, p in enumerate(phases):
        pes = ExitStack()
        Emit.SERIAL = (p in SERIAL_PHASES)
        PHASE_FNS[p](em, g, pes)
        em.barrier()
        pes.close()
    em.finalize()
    es.close()
    return nc, g


def kernel(**inputs):
    phases = PHASES or PHASE_ORDER
    consts = host_prepare(inputs)
    x = np.asarray(inputs["L0_attn_x"])
    percore = [{"xT": np.ascontiguousarray(x[c].T)} for c in range(NCORE)]

    if FUSED:
        nc, g = _build(phases, True)
        in_maps = []
        for c in range(NCORE):
            m = {}
            for name in g.kinds:
                if g.kinds[name] != "ExternalInput":
                    continue
                m[name] = percore[c].get(name, consts.get(name))
                assert m[name] is not None, name
            in_maps.append(m)
        res = run_bass_kernel_spmd(nc, in_maps, core_ids=list(range(NCORE)))
        outs = [r["outT"] for r in res.results]
        return np.stack([o.T for o in outs]).astype(np.float32)

    cores = CORES if CORES is not None else list(range(NCORE))
    env = [dict(percore[c]) for c in cores]
    final = None
    for p in phases:
        nc, g = _build([p], False)
        in_maps = []
        for i, c in enumerate(cores):
            m = {}
            for name, kind in g.kinds.items():
                if kind != "ExternalInput":
                    continue
                v = env[i].get(name)
                if v is None:
                    v = consts.get(name)
                assert v is not None, f"missing input {name} for phase {p}"
                m[name] = v
            in_maps.append(m)
        res = run_bass_kernel_spmd(nc, in_maps, core_ids=cores)
        for i in range(len(cores)):
            for name, kind in g.kinds.items():
                if kind == "ExternalOutput":
                    env[i][name] = np.asarray(res.results[i][name])
        final = [env[i].get("outT") for i in range(len(cores))]
    if final is not None and final[0] is not None:
        return np.stack([o.T for o in final]).astype(np.float32)
    return env
